# version 21
# baseline (speedup 1.0000x reference)
import numpy as np
import concourse.bass as bass
import concourse.mybir as mybir
from concourse.bass_utils import run_bass_kernel_spmd

F32 = mybir.dt.float32
BF16 = mybir.dt.bfloat16
ALU = mybir.AluOpType
ACT = mybir.ActivationFunctionType
AX = mybir.AxisListType

NCORES = 8
NSEQ = 2
L = 2048
D = 1024
DC = 8
NH = 16
TILE = 512
NT = L // TILE
TT = 128
NTT = L // TT
FT = 256
NFT = L // FT
JG = 4
DS = 16
EPS_LN = 1e-5
EPS_AT = 1e-6

N_SUBLAYERS = 4
DEBUG = False
DBG_OUT = {}


class Prog:
    ENGS = ("v", "s", "t", "g", "y")

    def __init__(self):
        self.plan = {e: [] for e in self.ENGS}
        self.cnt = {"v": 0, "s": 0, "t": 0, "g": 0, "ld": 0, "ls": 0, "lg": 0, "so": 0}
        self.pending = {e: [] for e in self.ENGS}

    def step(self, eng, emit, deps=(), ndma=0, dmasem="ld"):
        flat = []
        for d in deps:
            if d is None:
                continue
            if isinstance(d, list):
                flat.extend(d)
            else:
                flat.append(d)
        deps = flat + self.pending[eng]
        self.pending[eng] = []
        if ndma:
            self.cnt[dmasem] += 16 * ndma
            tok = (dmasem, self.cnt[dmasem])
            self.plan[eng].append((deps, emit, None))
        else:
            self.cnt[eng] += 1
            tok = (eng, self.cnt[eng])
            self.plan[eng].append((deps, emit, tok))
        return tok

    def snap(self, sem):
        return (sem, self.cnt[sem])

    def barrier(self):
        toks = [(k, v) for k, v in self.cnt.items() if v > 0]
        for e in self.ENGS:
            self.pending[e] = list(toks)


def build_nc():
    nc = bass.Bass("TRN2", target_bir_lowering=False, dynamic_dma_scratch_size=256)

    def di(name, shape):
        return nc.dram_tensor(name, shape, F32, kind="ExternalInput").ap()
    x_h = di("x", [NSEQ, DC, 128, L])
    wq_h = di("wq", [D, D]); wk_h = di("wk", [D, D])
    wv_h = di("wv", [D, D]); wo_h = di("wo", [D, D])
    w1_h = di("w1", [D, 4 * D]); w2_h = di("w2", [4 * D, D])
    wt_h = di("wt", [D, D])
    wxp_h = di("wxp", [128, 768])
    wdt_h = di("wdt", [64, D])
    alog_h = di("alog", [128, 128])
    cnames = ["bq", "bo", "b2", "g1", "be1", "g2", "be2", "bdt", "dvec", "biast"]
    vecs = {n: di(n, [128, 8]) for n in cnames}
    vecs["bk"] = di("bk", [D]); vecs["bv"] = di("bv", [D])
    b1_h = di("b1", [128, 32])
    alpha_h = di("alpha", [1])
    out_h = nc.dram_tensor("o", [NSEQ, DC, 128, L], F32, kind="ExternalOutput").ap()
    dbg_h = {}

    ctx = []
    def sb(name, shape, dt=F32):
        h = nc.sbuf_tensor(name, shape, dt); t = h.__enter__(); ctx.append(h); return t
    def psum(name, shape, dt=F32):
        h = nc.psum_tensor(name, shape, dt); t = h.__enter__(); ctx.append(h); return t
    def semh(name):
        h = nc.semaphore(name); s = h.__enter__(); ctx.append(h); return s

    xA = sb("xA", [128, DC * L])
    xB = sb("xB", [128, DC * L])
    lnD = sb("lnD", [128, 6144], BF16)
    wW = sb("wW", [128, 2 * 8192], BF16)
    sc1 = sb("sc1", [128, 2048])
    sc2 = sb("sc2", [128, 1024])
    qT = sb("qT", [128, 4096], BF16)
    eT = sb("eT", [128, 4096], BF16)
    zb = sb("zb", [128, 1024])
    nmu_bc = sb("nmu_bc", [128, 1024])
    rsd_bc = sb("rsd_bc", [128, 1024])
    rows2 = sb("rows2", [128, 2048])
    colp = sb("colp", [128, 152])
    zxsel = sb("zxsel", [128, 128])
    Acol = sb("Acol", [128, 128])
    kvE = sb("kvE", [128, 512], BF16)
    ksE = sb("ksE", [128, 16], BF16)
    onesc = sb("onesc", [128, 1], BF16)
    onesr = sb("onesr", [1, 128], BF16)
    zrow = sb("zrow", [1, 128], BF16)
    CB = {"bq": 8, "negbq": 16, "bo": 24, "b2": 32, "g1": 40, "be1": 48,
          "g2": 56, "be2": 64, "dvec": 72, "bdt": 80, "biast": 88}

    pW0 = psum("pW0", [128, 1024])
    pW1 = psum("pW1", [128, 1024])
    pKV = psum("pKV", [128, 512])
    pKs = psum("pKs", [128, 8])
    pDen = psum("pDen", [2, 512])
    pSt = psum("pSt", [128, 512])

    sems = {n: semh(n) for n in ("v", "s", "t", "g", "ld", "ls", "lg", "so")}
    prog = Prog()

    def V(emit, *deps): return prog.step("v", emit, deps)
    def S(emit, *deps): return prog.step("s", emit, deps)
    def T(emit, *deps): return prog.step("t", emit, deps)
    def G(emit, *deps): return prog.step("g", emit, deps)
    def DMA(emit, *deps): return prog.step("y", emit, deps, ndma=1)
    def DMAS(emit, *deps): return prog.step("s", emit, deps, ndma=1, dmasem="ls")
    def DMAG(emit, *deps): return prog.step("g", emit, deps, ndma=1, dmasem="lg")
    def DMAO(emit, *deps): return prog.step("y", emit, deps, ndma=1, dmasem="so")
    LD = sems["ld"]; LS = sems["ls"]; SO = sems["so"]; LG = sems["lg"]

    def tap(name, ap, dt, *deps):
        if not DEBUG or name in dbg_h:
            return
        shape = list(ap.shape)
        dbg_h[name] = nc.dram_tensor("dbg_" + name, shape, dt, kind="ExternalOutput").ap()
        DMAO(lambda e, ap=ap, name=name: e.dma_start(out=dbg_h[name], in_=ap).then_inc(SO, 16),
             *deps)

    sc2bf = sc2[:, :].bitcast(BF16)
    qTf32 = qT[:, :].bitcast(F32)
    eTf32 = eT[:, :].bitcast(F32)
    zbbf = zb[:, :].bitcast(BF16)
    dtlow_bf = zbbf[0:64, 0:2048]

    def bcast(row_ap, p=128, n=L):
        return row_ap.unsqueeze(1).broadcast_to([row_ap.shape[0], p, n])

    def wslot(i):
        return wW[:, i * 8192:(i + 1) * 8192]

    def wchunk(slot, c, a, b):
        return wslot(slot)[:, c * 1024 + a: c * 1024 + b]

    wxpv = wslot(0)[:, 0:768]
    wdtv = wslot(0)[0:64, 1024:2048]

    def ld_col(nm, cc):
        DMA(lambda e, nm=nm, cc=cc: e.dma_start(
            out=colp[:, cc:cc + 8], in_=vecs[nm]).then_inc(LD, 16))
    for nm, cc in [("bq", 8), ("bo", 24), ("b2", 32), ("g1", 40), ("be1", 48),
                   ("g2", 56), ("be2", 64), ("dvec", 72), ("bdt", 80), ("biast", 88)]:
        ld_col(nm, cc)
    DMA(lambda e: e.dma_start(out=colp[:, 96:128], in_=b1_h).then_inc(LD, 16))
    DMA(lambda e: e.dma_start(
        out=colp[:, 128:129],
        in_=bcast(alpha_h.rearrange("(a b) -> a b", a=1), 128, 1)).then_inc(LD, 16))
    DMA(lambda e: e.dma_start(out=Acol[:, :], in_=alog_h).then_inc(LD, 16))
    tok_ld0 = prog.snap("ld")

    V(lambda e: e.memset(onesc[:], 1.0))
    V(lambda e: e.memset(colp[:, 131:132], 1.0))
    V(lambda e: e.memset(onesr[:], 1.0))
    V(lambda e: e.memset(zrow[:], 0.0))
    V(lambda e: e.memset(colp[:, 1:2], EPS_LN))
    V(lambda e: e.memset(colp[:, 3:4], EPS_AT))
    V(lambda e: e.memset(colp[:, 129:130], 1.0))
    V(lambda e: e.tensor_scalar(out=colp[:, 16:24], in0=colp[:, 8:16],
                                scalar1=-1.0, scalar2=None, op0=ALU.mult), tok_ld0)
    t_ea = S(lambda e: e.activation(out=sc1[:, 0:128], in_=Acol[:, :], func=ACT.Exp), tok_ld0)
    V(lambda e: e.tensor_scalar(out=Acol[:, :], in0=sc1[:, 0:128],
                                scalar1=-1.0, scalar2=None, op0=ALU.mult), t_ea)
    prog.barrier()

    sc1_war = [None, None]
    wcast = {0: [None] * 8, 1: [None] * 8}

    def load_weight_1024(slot, hbm, rows0=0, cols0=0, dep0=None):
        war = prog.snap("t")
        last = None
        half = [sc1_war[0], sc1_war[1]]
        for c in range(8):
            sl = c % 2
            deps = [half[sl], dep0 if c < 2 else None]
            if sl == 0:
                def em(e, c=c, sl=sl, hbm=hbm, rows0=rows0, cols0=cols0):
                    return e.dma_start(out=sc1[:, sl * 1024:(sl + 1) * 1024],
                                       in_=hbm[rows0 + c * 128: rows0 + (c + 1) * 128,
                                               cols0:cols0 + 1024]).then_inc(LD, 16)
                DMA(em, *deps)
                tdma = prog.snap("ld")
            else:
                def em(e, c=c, sl=sl, hbm=hbm, rows0=rows0, cols0=cols0):
                    return e.dma_start(out=sc1[:, sl * 1024:(sl + 1) * 1024],
                                       in_=hbm[rows0 + c * 128: rows0 + (c + 1) * 128,
                                               cols0:cols0 + 1024]).then_inc(LS, 16)
                DMAS(em, *deps)
                tdma = prog.snap("ls")
            def ec(e, c=c, sl=sl, slot=slot):
                return e.activation(out=wchunk(slot, c, 0, 1024),
                                    in_=sc1[:, sl * 1024:(sl + 1) * 1024], func=ACT.Identity)
            tcast = S(ec, tdma, war if c == 0 else None)
            half[sl] = tcast
            last = tcast
            wcast[slot][c] = tcast
        sc1_war[0] = half[0]
        sc1_war[1] = half[1]
        return last

    pst_war = [None]

    def ln_stats(src, alpha_scale=False, wdeps=()):
        sxr = qTf32[0:1, 0:2048]
        sqr = eTf32[0:1, 0:2048]
        m2r = sc1[0:1, 0:2048]
        last_mm = None
        last_ev = None
        for piece in range(8):
            o = piece * 256
            for c in range(DC):
                def ec(e, o=o, c=c):
                    return e.tensor_copy(out=sc2bf[:, 0:256], in_=src(c)[:, o:o + 256])
                deps = list(wdeps) if (piece == 0 and c == 0) else []
                if last_mm is not None:
                    deps.append(last_mm)
                tc = V(ec, *deps)
                tsq = S(lambda e: e.activation(out=sc2bf[:, 256:512], in_=sc2bf[:, 0:256],
                                               func=ACT.Square), tc)
                def em(e, c=c):
                    return e.matmul(out=pSt[0:1, :], lhsT=onesc[:, :], rhs=sc2bf[:, 0:512],
                                    start=(c == 0), stop=(c == DC - 1))
                last_mm = T(em, tsq, last_ev if c == 0 else None,
                            pst_war[0] if (piece == 0 and c == 0) else None)
            def ev1(e, o=o):
                return e.tensor_copy(out=sxr[:, o:o + 256], in_=pSt[0:1, 0:256])
            def ev2(e, o=o):
                return e.tensor_copy(out=sqr[:, o:o + 256], in_=pSt[0:1, 256:512])
            t1 = V(ev1, last_mm)
            last_ev = V(ev2, t1)
            pst_war[0] = last_ev
        tnm = V(lambda e: e.tensor_scalar(out=sxr, in0=sxr,
                                          scalar1=-1.0 / D, scalar2=None, op0=ALU.mult), last_ev)
        tm2 = S(lambda e: e.activation(out=m2r, in_=sxr, func=ACT.Square), tnm)
        tms = V(lambda e: e.tensor_scalar(out=sqr, in0=sqr,
                                          scalar1=1.0 / D, scalar2=None, op0=ALU.mult), last_ev)
        tvr = V(lambda e: e.tensor_tensor(out=sqr, in0=sqr, in1=m2r, op=ALU.subtract), tms, tm2)
        tln = S(lambda e: e.activation(out=sqr, in_=sqr, func=ACT.Ln,
                                       bias=colp[0:1, 1:2]), tvr)
        trs = S(lambda e: e.activation(out=sqr, in_=sqr, func=ACT.Exp, scale=-0.5), tln)
        tfin = trs
        if alpha_scale:
            tfin = V(lambda e: e.tensor_scalar(out=sqr, in0=sqr,
                                               scalar1=colp[0:1, 128:129], scalar2=None,
                                               op0=ALU.mult), trs)
        DMA(lambda e: e.dma_start(out=rows2[32:33, :], in_=sxr).then_inc(LD, 16), tnm, tfin)
        DMA(lambda e: e.dma_start(out=rows2[64:65, :], in_=sqr).then_inc(LD, 16), tfin)
        tok = prog.snap("ld")
        return (tok, tok)

    class BC:
        def __init__(self, rows_tok, global_war):
            self.rows_tok = rows_tok
            self.war = {0: global_war, 1: global_war}
            self.tok = {}

        def load(self, ti):
            half = ti % 2
            o = (ti * 512) % L
            def m1(e, o=o):
                return e.matmul(out=pSt[:, :], lhsT=zxsel[32:33, 0:128],
                                rhs=rows2[32:33, o:o + 512], start=True, stop=True)
            t1 = T(m1, self.rows_tok[0], self.rows_tok[1], pst_war[0], tok_ones)
            def s1(e, half=half):
                return e.activation(out=nmu_bc[:, half * 512:(half + 1) * 512],
                                    in_=pSt[:, :], func=ACT.Identity)
            ts1 = S(s1, t1, self.war[half])
            def m2(e, o=o):
                return e.matmul(out=pSt[:, :], lhsT=zxsel[64:65, 0:128],
                                rhs=rows2[64:65, o:o + 512], start=True, stop=True)
            t2 = T(m2, ts1)
            def s2(e, half=half):
                return e.activation(out=rsd_bc[:, half * 512:(half + 1) * 512],
                                    in_=pSt[:, :], func=ACT.Identity)
            ts2 = S(s2, t2, self.war[half])
            pst_war[0] = ts2
            self.tok[ti] = [ts1, ts2]
            return self.tok[ti]

        def nmu(self, ti, a=0, b=512):
            half = ti % 2
            return nmu_bc[:, half * 512 + a: half * 512 + b]

        def rsd(self, ti, a=0, b=512):
            half = ti % 2
            return rsd_bc[:, half * 512 + a: half * 512 + b]

        def done(self, ti, tok):
            self.war[ti % 2] = tok

    def pclear(region_ap, ncols, *deps):
        t = None
        for k in range(0, ncols, 128):
            w = min(128, ncols - k)
            def em(e, k=k, w=w, region_ap=region_ap):
                return e.matmul(out=region_ap[:, k:k + w], lhsT=zrow[0:1, 0:128],
                                rhs=onesr[0:1, 0:w], start=True, stop=False)
            t = T(em, *deps)
        return t

    def load_kv_bias_rows(dep0):
        DMA(lambda e: e.dma_start(out=sc2[0:1, 0:1024],
                                  in_=vecs["bk"].rearrange("(a b) -> a b", a=1)).then_inc(LD, 16),
            dep0)
        t1 = V(lambda e: e.tensor_copy(out=zbbf[0:1, 0:1024], in_=sc2[0:1, 0:1024]),
               prog.snap("ld"))
        DMA(lambda e: e.dma_start(out=sc2[0:1, 0:1024],
                                  in_=vecs["bv"].rearrange("(a b) -> a b", a=1)).then_inc(LD, 16),
            t1)
        t2 = V(lambda e: e.tensor_copy(out=zbbf[0:1, 1024:2048], in_=sc2[0:1, 0:1024]),
               prog.snap("ld"))
        return t2

    xap = lambda c: xA[:, c * L:(c + 1) * L]
    xbap = lambda c: xB[:, c * L:(c + 1) * L]

    def xa2bf(c, a, b):
        if c < 2:
            return qT[:, c * 2048 + a: c * 2048 + b]
        if c < 4:
            return eT[:, (c - 2) * 2048 + a: (c - 2) * 2048 + b]
        if c == 4:
            return zbbf[:, a:b]
        if c < 7:
            return lnD[:, 2048 + (c - 5) * 2048 + a: 2048 + (c - 5) * 2048 + b]
        return sc2bf[:, a:b]

    tzx1 = V(lambda e: e.memset(zxsel[0:1, 0:64], 1.0))
    tzx2 = V(lambda e: e.memset(zxsel[0:1, 64:128], 0.0), tzx1)
    V(lambda e: e.memset(zxsel[32:33, 0:128], 1.0))
    V(lambda e: e.memset(zxsel[64:65, 0:128], 1.0))
    tok_ones = prog.snap("v")
    DMA(lambda e: e.dma_start(out=zxsel[1:2, 64:128],
                              in_=zxsel[0:1, 0:64]).then_inc(LD, 16), tzx2)
    DMA(lambda e: e.dma_start(out=zxsel[1:2, 0:64],
                              in_=zxsel[0:1, 64:128]).then_inc(LD, 16))
    tok_zxsel = prog.snap("ld")

    for seq in range(NSEQ):
        for c in range(DC):
            DMA(lambda e, c=c, seq=seq: e.dma_start(
                out=xA[:, c * L:(c + 1) * L], in_=x_h[seq, c]).then_inc(LD, 16))
        tok_x = prog.snap("ld")

        rt0 = ln_stats(xap, wdeps=[tok_x])
        tok_bkv = load_kv_bias_rows(rt0[1])
        tap("rows2", rows2[:, :], F32, rt0[0], rt0[1])
        tok_wk = load_weight_1024(0, wk_h, dep0=rt0[1])
        tok_wv = load_weight_1024(1, wv_h)
        tok_ckv = pclear(pKV[:, :], 512)
        tok_cks = pclear(pKs[:, :], 8)
        bc0 = BC(rt0, None)

        kv_mm_tok = None
        last_kevac = None
        last_vevac = None
        xn_war = {}
        for tt in range(NTT):
            o = tt * TT
            ti = tt // 4
            if tt % 4 == 0:
                bc0.load(ti)
            tok_bc = bc0.tok[ti]
            xn_tt = eT[:, 2048 + (tt % 2) * 1024:3072 + (tt % 2) * 1024]
            t_xn = None
            for c in range(DC):
                oin = (tt % 4) * 128
                def e1(e, c=c, o=o, ti=ti, oin=oin):
                    return e.tensor_tensor(out=sc2[:, 0:128], in0=xap(c)[:, o:o + TT],
                                           in1=bc0.nmu(ti, oin, oin + 128), op=ALU.add)
                deps = [tok_bc]
                if c == 0:
                    deps.append(xn_war.get(tt % 2))
                d1 = V(e1, *deps)
                def e2(e, c=c, ti=ti, oin=oin, xn_tt=xn_tt):
                    return e.tensor_tensor(out=xn_tt[:, c * 128:(c + 1) * 128], in0=sc2[:, 0:128],
                                           in1=bc0.rsd(ti, oin, oin + 128), op=ALU.mult)
                t_xn = V(e2, d1)
            bc0.done(ti, t_xn)
            tok_xn = t_xn
            if tt == 0:
                tap("xn_tt0", eT[:, 2048:3072], BF16, tok_xn)
            mm_last = None
            for proj, slot, pdst, wtok in (("k", 0, pW0, tok_wk), ("v", 1, pW1, tok_wv)):
                for h in range(2):
                    for c in range(DC):
                        def em(e, c=c, h=h, slot=slot, pdst=pdst, xn_tt=xn_tt):
                            return e.matmul(out=pdst[:, h * 512:(h + 1) * 512],
                                            lhsT=xn_tt[:, c * 128:(c + 1) * 128],
                                            rhs=wchunk(slot, c, h * 512, (h + 1) * 512),
                                            start=(c == 0), stop=False)
                        deps = [wcast[slot][c]]
                        if c == 0:
                            deps += [tok_xn,
                                     last_kevac if slot == 0 else last_vevac]
                        mm_last = T(em, *deps)
                    def eb(e, h=h, proj=proj, pdst=pdst):
                        off = 0 if proj == "k" else 1024
                        return e.matmul(out=pdst[:, h * 512:(h + 1) * 512],
                                        lhsT=onesr[0:1, 0:128],
                                        rhs=zbbf[0:1, off + h * 512: off + (h + 1) * 512],
                                        start=False, stop=True)
                    deps = [tok_bkv] if tt == 0 else []
                    mm_last = T(eb, *deps)
            tok_kvproj = mm_last
            k_tm = eT[:, 0:1024]
            v_tm = eT[:, 1024:2048]
            ts1 = S(lambda e: e.activation(out=qT[:, 0:1024], in_=pW0[:, :], func=ACT.Relu),
                    tok_kvproj)
            ts3 = None
            for h2 in range(2):
                def es2(e, h2=h2):
                    return e.activation(out=sc2[:, 512:1024],
                                        in_=pW0[:, h2 * 512:(h2 + 1) * 512],
                                        func=ACT.Relu, scale=-1.0)
                ts2 = S(es2, tok_kvproj if h2 == 0 else ts3)
                def es3(e, h2=h2):
                    return e.activation(out=qT[:, 1024 + h2 * 512:1536 + h2 * 512],
                                        in_=sc2[:, 512:1024], func=ACT.Exp, scale=-1.0)
                ts3 = S(es3, ts2)
            tk = V(lambda e: e.tensor_tensor(out=k_tm[:, :], in0=qT[:, 0:1024],
                                             in1=qT[:, 1024:2048], op=ALU.add),
                   ts1, ts3, kv_mm_tok)
            last_kevac = tk
            tv = S(lambda e: e.activation(out=v_tm[:, :], in_=pW1[:, :], func=ACT.Identity),
                   tok_kvproj, kv_mm_tok)
            last_vevac = tv
            if tt == 0:
                tap("kv_tt0", eT[:, 0:2048], BF16, tk, tv)
            for hh in range(NH):
                def ekv(e, hh=hh):
                    return e.matmul(out=pKV[64 * (hh % 2):64 * (hh % 2) + 64,
                                          (hh // 2) * 64:(hh // 2) * 64 + 64],
                                    lhsT=k_tm[:, hh * 64:(hh + 1) * 64],
                                    rhs=v_tm[:, hh * 64:(hh + 1) * 64],
                                    start=False, stop=(tt == NTT - 1))
                deps = [tk, tv] if hh == 0 else []
                if hh == 0 and tt == 0:
                    deps += [tok_ckv, tok_cks]
                T(ekv, *deps)
                def eks(e, hh=hh):
                    return e.matmul(out=pKs[64 * (hh % 2):64 * (hh % 2) + 64,
                                          (hh // 2):(hh // 2) + 1],
                                    lhsT=k_tm[:, hh * 64:(hh + 1) * 64],
                                    rhs=onesc[:, :],
                                    start=False, stop=(tt == NTT - 1))
                kv_mm_tok = T(eks)
            xn_war[tt % 2] = kv_mm_tok
        tkve = V(lambda e: e.tensor_copy(out=kvE[:, :], in_=pKV[:, :]), kv_mm_tok)
        tks0 = V(lambda e: e.memset(ksE[:, :], 0.0))
        tok_kv = tkve
        for hh in range(NH):
            def ek(e, hh=hh):
                return e.tensor_copy(out=ksE[64 * (hh % 2):64 * (hh % 2) + 64, hh:hh + 1],
                                     in_=pKs[64 * (hh % 2):64 * (hh % 2) + 64,
                                             (hh // 2):(hh // 2) + 1])
            tok_kv = V(ek, kv_mm_tok, tks0)
        tap("kvE", kvE[:, :], BF16, tok_kv)
        tap("ksE", ksE[:, :], BF16, tok_kv)
        prog.barrier()

        tok_wq = load_weight_1024(0, wq_h)
        tok_wo = load_weight_1024(1, wo_h)

        zmm_rd = None
        bc2 = BC(rt0, tok_xn)
        last_wo_ev = None
        for ti in range(NT):
            o = ti * TILE
            bc2.load(ti)
            tok_bc = bc2.tok[ti]
            xn_bf = eT
            t_xn = None
            for c in range(DC):
                def e1(e, c=c, o=o, ti=ti):
                    return e.tensor_tensor(out=sc2[:, 0:512], in0=xap(c)[:, o:o + TILE],
                                           in1=bc2.nmu(ti), op=ALU.add)
                deps = [tok_bc]
                if ti > 0 and c == 0:
                    deps.append(last_wo_ev)
                d1 = V(e1, *deps)
                def e2(e, c=c, ti=ti):
                    return e.tensor_tensor(out=xn_bf[:, c * 512:(c + 1) * 512], in0=sc2[:, 0:512],
                                           in1=bc2.rsd(ti), op=ALU.mult)
                t_xn = V(e2, d1)
            bc2.done(ti, t_xn)
            tok_xn = t_xn
            q_rd = {}
            tok_q = None
            for m in range(DC):
                half = m % 2
                for c in range(DC):
                    def em(e, m=m, c=c, half=half):
                        return e.matmul(out=pW0[:, half * 512:(half + 1) * 512],
                                        lhsT=wchunk(0, c, m * 128, (m + 1) * 128),
                                        rhs=xn_bf[:, c * 512:(c + 1) * 512],
                                        start=(c == 0), stop=(c == DC - 1))
                    deps = [wcast[0][c]]
                    if c == 0:
                        deps.append(tok_xn)
                        if m >= 2:
                            deps.append(q_rd[m - 2])
                    tq = T(em, *deps)
                def s1(e, half=half, m=m):
                    return e.activation(out=sc2bf[:, 0:512],
                                        in_=pW0[:, half * 512:(half + 1) * 512],
                                        func=ACT.Relu, bias=colp[:, CB["bq"] + m:CB["bq"] + m + 1])
                a1 = S(s1, tq, tok_q)
                def s2(e, half=half, m=m):
                    return e.activation(out=sc2[:, 256:768],
                                        in_=pW0[:, half * 512:(half + 1) * 512],
                                        func=ACT.Relu, scale=-1.0,
                                        bias=colp[:, CB["negbq"] + m:CB["negbq"] + m + 1])
                a2 = S(s2, a1)
                a3 = S(lambda e: e.activation(out=sc2bf[:, 1536:2048], in_=sc2[:, 256:768],
                                              func=ACT.Exp, scale=-1.0), a2)
                def v1(e, m=m):
                    return e.tensor_tensor(out=qT[:, m * 512:(m + 1) * 512],
                                           in0=sc2bf[:, 0:512], in1=sc2bf[:, 1536:2048],
                                           op=ALU.add)
                tok_q = V(v1, a3)
                q_rd[m] = a2
            if ti == 0:
                tap("q_t0", qT[:, :], BF16, tok_q)
            den_ev = {}
            zb_tok = {}
            zbd_tok = {}
            o_ev = []
            for c in range(DC):
                half = c % 2
                dhalf = c % 2
                def ed(e, c=c):
                    return e.matmul(out=pDen[:, :], lhsT=ksE[:, 2 * c:2 * c + 2],
                                    rhs=qT[:, c * 512:(c + 1) * 512], start=True, stop=True)
                deps = [tok_q, tok_kv]
                if c >= 1:
                    deps.append(den_ev[c - 1])
                td = T(ed, *deps)
                def ev(e, dhalf=dhalf):
                    return e.tensor_copy(out=sc2[0:2, dhalf * 512:(dhalf + 1) * 512],
                                         in_=pDen[:, :])
                deps2 = [td]
                if c >= 2:
                    deps2.append(zb_tok[c - 2])
                    deps2.append(zbd_tok[c - 2])
                tde = V(ev, *deps2)
                den_ev[c] = tde
                def ez1(e, dhalf=dhalf):
                    return e.activation(out=sc2[0:2, dhalf * 512:(dhalf + 1) * 512],
                                        in_=sc2[0:2, dhalf * 512:(dhalf + 1) * 512],
                                        func=ACT.Ln, bias=colp[0:2, 3:4])
                tz1 = S(ez1, tde)
                def ez2(e, dhalf=dhalf):
                    return e.activation(out=sc2[0:2, dhalf * 512:(dhalf + 1) * 512],
                                        in_=sc2[0:2, dhalf * 512:(dhalf + 1) * 512],
                                        func=ACT.Exp, scale=-1.0)
                tz2 = S(ez2, tz1)
                tmo = None
                for hh in (2 * c, 2 * c + 1):
                    def eo(e, hh=hh, c=c, half=half):
                        return e.matmul(out=pW1[64 * (hh % 2):64 * (hh % 2) + 64,
                                              half * 512:(half + 1) * 512],
                                        lhsT=kvE[64 * (hh % 2):64 * (hh % 2) + 64,
                                                 (hh // 2) * 64:(hh // 2) * 64 + 64],
                                        rhs=qT[64 * (hh % 2):64 * (hh % 2) + 64,
                                               c * 512:(c + 1) * 512],
                                        start=True, stop=True)
                    deps = [tok_kv, tok_q]
                    if len(o_ev) >= 2:
                        deps.append(o_ev[-2])
                    tmo = T(eo, *deps)
                def ezm(e, dhalf=dhalf):
                    return e.matmul(out=pKV[:, 0:512], lhsT=zxsel[0:2, 0:128],
                                    rhs=sc2[0:2, dhalf * 512:(dhalf + 1) * 512],
                                    start=True, stop=True)
                t_zmm = T(ezm, tz2, tok_zxsel, tok_kv, zmm_rd)
                zb_tok[c] = t_zmm
                zbd_tok[c] = t_zmm
                def ezev(e, half=half):
                    return e.activation(out=zb[:, half * 512:(half + 1) * 512],
                                        in_=pKV[:, 0:512], func=ACT.Identity)
                t_zev = S(ezev, t_zmm, o_ev[-2] if len(o_ev) >= 2 else None)
                zmm_rd = t_zev
                def evv(e, c=c, half=half):
                    return e.tensor_tensor(out=qT[:, c * 512:(c + 1) * 512],
                                           in0=pW1[:, half * 512:(half + 1) * 512],
                                           in1=zb[:, half * 512:(half + 1) * 512], op=ALU.mult)
                o_ev.append(V(evv, tmo, t_zev))
            tok_osc = o_ev[-1]
            if ti == 0:
                tap("osc_t0", qT[:, :], BF16, tok_osc)
            wo_rd = {}
            for m in range(DC):
                half = m % 2
                for c in range(DC):
                    def em(e, m=m, c=c, half=half):
                        return e.matmul(out=pW0[:, half * 512:(half + 1) * 512],
                                        lhsT=wchunk(1, c, m * 128, (m + 1) * 128),
                                        rhs=qT[:, c * 512:(c + 1) * 512],
                                        start=(c == 0), stop=(c == DC - 1))
                    deps = [wcast[1][c]]
                    if c == 0:
                        deps.append(tok_osc)
                        deps.append(wo_rd[m - 2] if m >= 2 else q_rd[6 + m])
                    tw = T(em, *deps)
                def s1(e, m=m, half=half):
                    return e.activation(out=sc2[:, 0:512], in_=pW0[:, half * 512:(half + 1) * 512],
                                        func=ACT.Identity,
                                        bias=colp[:, CB["bo"] + m:CB["bo"] + m + 1])
                aw = S(s1, tw, last_wo_ev)
                def v1(e, m=m, o=o):
                    return e.tensor_tensor(out=xbap(m)[:, o:o + TILE], in0=sc2[:, 0:512],
                                           in1=xn_bf[:, m * 512:(m + 1) * 512], op=ALU.add)
                last_wo_ev = V(v1, aw)
                wo_rd[m] = aw
        tap("xa_c0", xbap(0), F32, last_wo_ev)
        prog.barrier()

        rt1 = ln_stats(xbap)
        bc3 = BC(rt1, None)
        tok_w1 = load_weight_1024(0, w1_h, rows0=0, cols0=0, dep0=rt1[1])
        tok_w2 = load_weight_1024(1, w2_h, rows0=0, cols0=0, dep0=rt1[1])
        tok_xa2bf = None
        for ti in range(NT):
            o = ti * TILE
            bc3.load(ti)
            tok_bc = bc3.tok[ti]
            tlast = None
            for c in range(DC):
                d1 = V(lambda e, c=c, o=o, ti=ti: e.tensor_tensor(
                    out=xbap(c)[:, o:o + TILE], in0=xbap(c)[:, o:o + TILE],
                    in1=bc3.nmu(ti), op=ALU.add), tok_bc)
                d2 = V(lambda e, c=c, o=o, ti=ti: e.tensor_tensor(
                    out=xbap(c)[:, o:o + TILE], in0=xbap(c)[:, o:o + TILE],
                    in1=bc3.rsd(ti), op=ALU.mult), d1)
                d3 = V(lambda e, c=c, o=o: e.tensor_scalar(
                    out=xbap(c)[:, o:o + TILE], in0=xbap(c)[:, o:o + TILE],
                    scalar1=colp[:, CB["g1"] + c:CB["g1"] + c + 1],
                    scalar2=None, op0=ALU.mult), d2)
                d4 = V(lambda e, c=c, o=o: e.tensor_scalar(
                    out=xbap(c)[:, o:o + TILE], in0=xbap(c)[:, o:o + TILE],
                    scalar1=colp[:, CB["be1"] + c:CB["be1"] + c + 1],
                    scalar2=None, op0=ALU.add), d3)
                tlast = V(lambda e, c=c, o=o: e.tensor_copy(
                    out=xa2bf(c, o, o + TILE), in_=xbap(c)[:, o:o + TILE]), d4)
            bc3.done(ti, tlast)
            tok_xa2bf = tlast
        tap("xa2_c0", xbap(0), F32, tok_xa2bf)
        tok_xa2_done = tok_xa2bf
        tap("xa2bf0", qT[:, 0:2048], BF16, tok_xa2bf)
        tap("xa2bf4", zbbf[:, 0:2048], BF16, tok_xa2bf)
        tap("xa2bf5", lnD[:, 2048:4096], BF16, tok_xa2bf)
        tap("xa2bf7", sc2bf[:, 0:2048], BF16, tok_xa2bf)

        last_w2ev = None
        eT_war = None
        for jg in range(JG):
            if jg > 0:
                tok_w2 = load_weight_1024(1, w2_h, rows0=jg * 1024, cols0=0)
            for ti in range(NFT):
                o = ti * FT
                h_toks = {}
                tok_h1 = None
                for j in range(8):
                    half = j % 2
                    for c in range(DC):
                        def em(e, j=j, c=c, half=half, o=o):
                            return e.matmul(out=pW0[:, half * 512:half * 512 + FT],
                                            lhsT=wchunk(0, c, j * 128, (j + 1) * 128),
                                            rhs=xa2bf(c, o, o + FT),
                                            start=(c == 0), stop=(c == DC - 1))
                        deps = [wcast[0][c]]
                        if c == 0:
                            deps.append(tok_xa2bf)
                            if j >= 2:
                                deps.append(h_toks[j - 2])
                            elif last_w2ev is not None:
                                deps.append(last_w2ev)
                        T(em, *deps)
                    jglob = jg * 8 + j
                    def s1(e, j=j, half=half, jglob=jglob):
                        return e.activation(out=lnD[:, j * FT:(j + 1) * FT],
                                            in_=pW0[:, half * 512:half * 512 + FT],
                                            func=ACT.Relu,
                                            bias=colp[:, 96 + jglob:96 + jglob + 1])
                    tok_h1 = S(s1, prog.snap("t"))
                    h_toks[j] = tok_h1
                if ti == NFT - 1 and jg + 1 < JG:
                    load_weight_1024(0, w1_h, rows0=0, cols0=(jg + 1) * 1024)
                if jg == 0 and ti == 0:
                    tap("h1_00", lnD[:, 0:2048], BF16, tok_h1)
                for m in range(DC):
                    half = m % 2
                    for j in range(8):
                        def em(e, m=m, j=j, half=half):
                            return e.matmul(out=pW1[:, half * 512:half * 512 + FT],
                                            lhsT=wchunk(1, j, m * 128, (m + 1) * 128),
                                            rhs=lnD[:, j * FT:(j + 1) * FT],
                                            start=(j == 0), stop=(j == 7))
                        deps = [wcast[1][j]]
                        if j == 0:
                            deps.append(tok_h1)
                            if last_w2ev is not None and m >= 2:
                                deps.append(last_w2ev)
                        tw = T(em, *deps)
                    if jg == 0:
                        def sb2(e, m=m, half=half):
                            return e.activation(out=nmu_bc[:, 0:FT],
                                                in_=pW1[:, half * 512:half * 512 + FT],
                                                func=ACT.Identity,
                                                bias=colp[:, CB["b2"] + m:CB["b2"] + m + 1])
                        tsb = S(sb2, tw, last_w2ev)
                        def ev(e, m=m, o=o):
                            return e.tensor_tensor(out=xbap(m)[:, o:o + FT],
                                                   in0=nmu_bc[:, 0:FT],
                                                   in1=xbap(m)[:, o:o + FT], op=ALU.add)
                        last_w2ev = V(ev, tsb)
                    else:
                        def ev(e, m=m, o=o, half=half):
                            return e.tensor_tensor(out=xbap(m)[:, o:o + FT],
                                                   in0=pW1[:, half * 512:half * 512 + FT],
                                                   in1=xbap(m)[:, o:o + FT], op=ALU.add)
                        last_w2ev = V(ev, tw)
            tap("ffn_jg%d" % jg, xbap(0), F32, last_w2ev)
        tap("x2py_c0", xbap(0), F32, last_w2ev)
        prog.barrier()

        rt2 = ln_stats(xbap)
        bc4 = BC(rt2, None)
        for ti in range(NT):
            o = ti * TILE
            bc4.load(ti)
            tok_bc = bc4.tok[ti]
            tlast = None
            for c in range(DC):
                d1 = V(lambda e, c=c, o=o, ti=ti: e.tensor_tensor(
                    out=sc1[:, 0:512], in0=xbap(c)[:, o:o + TILE], in1=bc4.nmu(ti), op=ALU.add),
                    tok_bc)
                d2 = V(lambda e, ti=ti: e.tensor_tensor(
                    out=sc1[:, 0:512], in0=sc1[:, 0:512], in1=bc4.rsd(ti), op=ALU.mult), d1)
                d3 = V(lambda e, c=c: e.tensor_scalar(
                    out=sc1[:, 0:512], in0=sc1[:, 0:512],
                    scalar1=colp[:, CB["g2"] + c:CB["g2"] + c + 1],
                    scalar2=None, op0=ALU.mult), d2)
                d4 = V(lambda e, c=c: e.tensor_scalar(
                    out=sc1[:, 0:512], in0=sc1[:, 0:512],
                    scalar1=colp[:, CB["be2"] + c:CB["be2"] + c + 1],
                    scalar2=None, op0=ALU.add), d3)
                tlast = V(lambda e, c=c, o=o: e.tensor_tensor(
                    out=xap(c)[:, o:o + TILE], in0=xap(c)[:, o:o + TILE], in1=sc1[:, 0:512],
                    op=ALU.add), d4)
            bc4.done(ti, tlast)
        prog.barrier()

        def ternary_prep(rt):
            rtoks = []
            for c in range(DC):
                sl = c % 2
                deps = [rtoks[c - 2]] if c >= 2 else [rt[1], sc1_war[sl]]
                if sl == 0:
                    def em(e, c=c, sl=sl):
                        return e.dma_start(out=sc1[:, sl * 1024:(sl + 1) * 1024],
                                           in_=wt_h[c * 128:(c + 1) * 128, :]).then_inc(LD, 16)
                    DMA(em, *deps)
                    td = prog.snap("ld")
                else:
                    def em(e, c=c, sl=sl):
                        return e.dma_start(out=sc1[:, sl * 1024:(sl + 1) * 1024],
                                           in_=wt_h[c * 128:(c + 1) * 128, :]).then_inc(LS, 16)
                    DMAS(em, *deps)
                    td = prog.snap("ls")
                def er(e, c=c, sl=sl):
                    return e.tensor_reduce(out=colp[:, 136 + c:137 + c],
                                           in_=sc1[:, sl * 1024:(sl + 1) * 1024],
                                           axis=AX.X, op=ALU.add, apply_absolute_value=True)
                rtoks.append(V(er, td))
            tsum = V(lambda e: e.tensor_reduce(out=colp[:, 144:145], in_=colp[:, 136:144],
                                               axis=AX.X, op=ALU.add), rtoks[-1])
            tap("tern_redc", colp[:, 130:146], F32, tsum)
            tmm = T(lambda e: e.matmul(out=pDen[0:1, 0:1], lhsT=colp[:, 131:132],
                                       rhs=colp[:, 144:145],
                                       start=True, stop=True), tsum)
            tev = V(lambda e: e.tensor_scalar(out=colp[0:1, 135:136], in0=pDen[0:1, 0:1],
                                              scalar1=0.7 / (1024.0 * 1024.0), scalar2=None,
                                              op0=ALU.mult), tmm)
            DMA(lambda e: e.dma_start(out=colp[1:2, 135:136],
                                      in_=colp[0:1, 135:136]).then_inc(LD, 16), tev)
            tok_dd = prog.snap("ld")
            tdm = T(lambda e: e.matmul(out=pSt[:, 0:1], lhsT=zxsel[0:2, 0:128],
                                       rhs=colp[0:2, 135:136], start=True, stop=True),
                    tok_dd, pst_war[0], tok_zxsel)
            tok_d = S(lambda e: e.activation(out=colp[:, 130:131], in_=pSt[:, 0:1],
                                             func=ACT.Identity), tdm)
            pst_war[0] = tok_d
            last = None
            qtoks = []
            pq_war = {0: None, 1: None}
            for c in range(DC):
                sl = c % 2
                deps = [rtoks[6 + sl] if c < 2 else None, pq_war[sl]]
                if sl == 0:
                    def em(e, c=c, sl=sl):
                        return e.dma_start(out=sc1[:, sl * 1024:(sl + 1) * 1024],
                                           in_=wt_h[c * 128:(c + 1) * 128, :]).then_inc(LD, 16)
                    DMA(em, *deps)
                    td2 = prog.snap("ld")
                else:
                    def em(e, c=c, sl=sl):
                        return e.dma_start(out=sc1[:, sl * 1024:(sl + 1) * 1024],
                                           in_=wt_h[c * 128:(c + 1) * 128, :]).then_inc(LS, 16)
                    DMAS(em, *deps)
                    td2 = prog.snap("ls")
                t1 = S(lambda e, sl=sl: e.activation(out=eT[:, 0:1024],
                                                     in_=sc1[:, sl * 1024:(sl + 1) * 1024],
                                                     func=ACT.Sign), td2, last, rt[1])
                t2 = S(lambda e, sl=sl: e.activation(out=qTf32[:, 1024:2048],
                                                     in_=sc1[:, sl * 1024:(sl + 1) * 1024],
                                                     func=ACT.Abs), td2, rt[1])
                pq_war[sl] = [t1, t2]
                t3 = V(lambda e: e.tensor_scalar(out=eT[:, 1024:2048], in0=qTf32[:, 1024:2048],
                                                 scalar1=colp[:, 130:131], scalar2=None,
                                                 op0=ALU.is_gt), t2, tok_d)
                last = V(lambda e, c=c: e.tensor_tensor(out=wchunk(1, c, 0, 1024),
                                                        in0=eT[:, 0:1024],
                                                        in1=eT[:, 1024:2048],
                                                        op=ALU.mult), t1, t3)
                qtoks.append(last)
            sc1_war[0] = last
            sc1_war[1] = last
            return last, qtoks

        def ternary_sublayer(first):
            rt = ln_stats(xap, alpha_scale=True)
            tok_tw, qtoks = ternary_prep(rt) if first else (None, None)
            tap("tern_w", wslot(1)[:, 0:8192], BF16, tok_tw)
            tap("tern_rows", rows2[:, :], F32, rt[0], rt[1])
            tap("tern_delta", colp[:, 130:131], F32, prog.snap("v"))
            bct = BC(rt, None)
            last_ev = None
            for ti in range(NT):
                o = ti * TILE
                bct.load(ti)
                tok_bc = bct.tok[ti]
                tok_xn1 = None
                for c in range(DC):
                    def e1(e, c=c, o=o, ti=ti):
                        return e.tensor_tensor(out=sc2[:, 0:512], in0=xap(c)[:, o:o + TILE],
                                               in1=bct.nmu(ti), op=ALU.add)
                    deps = [tok_bc]
                    if ti > 0 and c == 0:
                        deps.append(last_ev)
                    d1 = V(e1, *deps)
                    def e2(e, c=c, ti=ti):
                        return e.tensor_tensor(out=eT[:, c * 512:(c + 1) * 512],
                                               in0=sc2[:, 0:512],
                                               in1=bct.rsd(ti), op=ALU.mult)
                    tok_xn1 = V(e2, d1)
                bct.done(ti, tok_xn1)
                if ti == 0:
                    tap("tern_xn1", eT[:, :], BF16, tok_xn1)
                ev_rd = {}
                for m in range(DC):
                    half = m % 2
                    for c in range(DC):
                        def em(e, m=m, c=c, half=half):
                            return e.matmul(out=pW0[:, half * 512:(half + 1) * 512],
                                            lhsT=wchunk(1, c, m * 128, (m + 1) * 128),
                                            rhs=eT[:, c * 512:(c + 1) * 512],
                                            start=(c == 0), stop=(c == DC - 1))
                        deps = [qtoks[c] if qtoks is not None else None]
                        if c == 0:
                            deps.append(tok_xn1)
                            if m >= 2:
                                deps.append(ev_rd[m - 2])
                        tb = T(em, *deps)
                    def sbias(e, half=half, m=m):
                        return e.activation(out=pW1[:, half * 512:(half + 1) * 512],
                                            in_=pW0[:, half * 512:(half + 1) * 512],
                                            func=ACT.Identity,
                                            bias=colp[:, CB["biast"] + m:CB["biast"] + m + 1])
                    ts_ = S(sbias, tb, ev_rd.get(m - 2))
                    def ev(e, m=m, o=o, half=half):
                        return e.tensor_tensor(out=xap(m)[:, o:o + TILE],
                                               in0=pW1[:, half * 512:(half + 1) * 512],
                                               in1=xap(m)[:, o:o + TILE], op=ALU.add)
                    last_ev = V(ev, ts_)
                    ev_rd[m] = last_ev
            prog.barrier()

        if N_SUBLAYERS >= 2:
            ternary_sublayer(first=True)

        if N_SUBLAYERS >= 3:
            rt3 = ln_stats(xap)
            war_t = prog.snap("t")
            DMA(lambda e: e.dma_start(out=sc1[:, 0:768], in_=wxp_h).then_inc(LD, 16), rt3[1],
                sc1_war[0])
            tok_wl = prog.snap("ld")
            DMA(lambda e: e.dma_start(out=sc1[0:64, 1024:2048], in_=wdt_h).then_inc(LD, 16),
                sc1_war[1])
            tok_wl2 = prog.snap("ld")
            tok_wxp = V(lambda e: e.tensor_copy(out=wxpv, in_=sc1[:, 0:768]), tok_wl, war_t)
            tok_wdt = V(lambda e: e.tensor_copy(out=wdtv, in_=sc1[0:64, 1024:2048]), tok_wl2)
            sc1_war[0] = tok_wxp
            sc1_war[1] = tok_wdt
            tsel0 = V(lambda e: e.memset(wW[64:96, 2048:6144], 0.0))
            DMA(lambda e: e.dma_start(out=wW[64:65, 2048:2176],
                                      in_=onesr[0:1, 0:128]).then_inc(LD, 16), tsel0)
            for k in (1, 2, 4, 8, 16):
                DMA(lambda e, k=k: e.dma_start(
                    out=wW[64 + k:64 + 2 * k, 2048 + 128 * k:2048 + 256 * k],
                    in_=wW[64:64 + k, 2048:2048 + 128 * k]).then_inc(LD, 16))
            tok_sel = prog.snap("ld")

            def selw(w):
                return wW[64:96, 2048 + 128 * w:2176 + 128 * w]

            bc5 = BC(rt3, None)
            xp_last = None
            for ti in range(NT):
                o = ti * TILE
                bc5.load(ti)
                tok_bc = bc5.tok[ti]
                half = ti % 2
                ln_toks = {}
                for c in range(DC):
                    def e1(e, c=c, o=o, ti=ti):
                        return e.tensor_tensor(out=sc2[:, 0:512], in0=xap(c)[:, o:o + TILE],
                                               in1=bc5.nmu(ti), op=ALU.add)
                    deps = [tok_bc]
                    if ti >= 2 and c == 0:
                        deps.append(xp_last)
                    d1 = V(e1, *deps)
                    def e2(e, c=c, ti=ti):
                        return e.tensor_tensor(out=lnD[:, c * 512:(c + 1) * 512],
                                               in0=sc2[:, 0:512], in1=bc5.rsd(ti), op=ALU.mult)
                    ln_toks[c] = V(e2, d1)
                bc5.done(ti, ln_toks[DC - 1])
                txp = None
                for c in range(DC):
                    def em(e, c=c, half=half):
                        return e.matmul(out=pW0[0:96, half * 512:(half + 1) * 512],
                                        lhsT=wxpv[:, c * 96:(c + 1) * 96],
                                        rhs=lnD[:, c * 512:(c + 1) * 512],
                                        start=(c == 0), stop=(c == DC - 1))
                    deps = [ln_toks[c], tok_wxp]
                    if ti >= 2 and c == 0:
                        deps.append(xp_last)
                    txp = T(em, *deps)
                tv1 = V(lambda e, o=o, half=half: e.tensor_copy(
                    out=dtlow_bf[:, o:o + TILE],
                    in_=pW0[0:64, half * 512:(half + 1) * 512]), txp, rt3[0], rt3[1])
                xp_last = V(lambda e, o=o, half=half: e.tensor_copy(
                    out=zbbf[64:96, o:o + TILE],
                    in_=pW0[64:96, half * 512:(half + 1) * 512]), tv1)
            tok_xp = xp_last
            prog.barrier()

            dtB_ = [xB[:, 0:L], xB[:, 5 * L:6 * L]]
            yB_ = [xB[:, L:L + 1024].bitcast(BF16),
                   xB[:, L + 1024:2 * L].bitcast(BF16)]
            AbB_ = [xB[:, 2 * L:3 * L], xB[:, 3 * L:4 * L]]
            hB_ = [xB[:, 4 * L:4 * L + 1024].bitcast(BF16),
                   xB[:, 4 * L + 1024:5 * L].bitcast(BF16)]
            tokB_ = [xB[:, 6 * L:6 * L + 1024].bitcast(BF16),
                     xB[:, 6 * L + 1024:7 * L].bitcast(BF16)]
            prB_ = [xB[:, 7 * L:7 * L + 1024].bitcast(BF16),
                    xB[:, 7 * L + 1024:8 * L].bitcast(BF16)]
            ln2c = lnD[:, 0:4096].bitcast(F32)
            uB = lnD[:, 4096:6144]
            zcol = sc2[:, 1023:1024]
            tz = V(lambda e: e.memset(zcol, 0.0))
            tap("Acol_t", Acol[:, :], F32, tz)
            bc6 = BC(rt3, None)
            slab_last = [None, None]
            pw0_rd = {0: None, 1: None}
            pw1_rd = {0: None, 1: None}
            scan_done = [None, None]
            prod_done = [None, None]
            yadd_done = [None, None]
            dt_rd = {}
            bc6_cnt = [0]
            dt_war = [None, None]
            y_war = [None, None]
            sp_war = [None, None]
            st = {}

            def prep_chunk(c):
                buf = c % 2
                dtB = dtB_[buf]
                yB = yB_[buf]
                ln_tok = None
                tok_dt = None
                for ti in range(NT):
                    o = ti * TILE
                    bidx = bc6_cnt[0]; bc6_cnt[0] += 1
                    bc6.load(bidx)
                    tok_bc = bc6.tok[bidx]
                    def e1(e, c=c, o=o, bidx=bidx):
                        return e.tensor_tensor(out=sc2[:, 0:512], in0=xap(c)[:, o:o + TILE],
                                               in1=bc6.nmu(bidx), op=ALU.add)
                    d1 = V(e1, tok_bc)
                    yield
                    def e2(e, o=o, bidx=bidx):
                        return e.tensor_tensor(out=ln2c[:, o:o + TILE],
                                               in0=sc2[:, 0:512], in1=bc6.rsd(bidx), op=ALU.mult)
                    ln_tok = V(e2, d1)
                    bc6.done(bidx, ln_tok)
                    idx = c * NT + ti
                    b0 = (ti % 2) * 1024
                    def em(e, c=c, o=o):
                        return e.matmul(out=pKV[:, 0:512],
                                        lhsT=wdtv[:, c * 128:(c + 1) * 128],
                                        rhs=dtlow_bf[:, o:o + TILE],
                                        start=True, stop=True)
                    deps = [tok_xp, tok_wdt]
                    if idx >= 1:
                        deps.append(dt_rd[idx - 1])
                    tm = T(em, *deps)
                    def s1(e, c=c, b0=b0):
                        return e.activation(out=sc1[:, b0:b0 + 512],
                                            in_=pKV[:, 0:512],
                                            func=ACT.Identity,
                                            bias=colp[:, CB["bdt"] + c:CB["bdt"] + c + 1])
                    a1 = S(s1, tm, sp_war[ti % 2])
                    dt_rd[idx] = a1
                    a2 = S(lambda e, b0=b0: e.activation(out=sc1[:, b0 + 512:b0 + 1024],
                                                         in_=sc1[:, b0:b0 + 512],
                                                         func=ACT.Abs), a1)
                    a3 = S(lambda e, b0=b0: e.activation(out=sc1[:, b0 + 512:b0 + 1024],
                                                         in_=sc1[:, b0 + 512:b0 + 1024],
                                                         func=ACT.Exp, scale=-1.0), a2)
                    a4 = S(lambda e, b0=b0: e.activation(out=sc1[:, b0 + 512:b0 + 1024],
                                                         in_=sc1[:, b0 + 512:b0 + 1024],
                                                         func=ACT.Ln, bias=colp[:, 129:130]), a3)
                    a5 = S(lambda e, b0=b0: e.activation(out=sc1[:, b0:b0 + 512],
                                                         in_=sc1[:, b0:b0 + 512],
                                                         func=ACT.Relu), a4)
                    def v1(e, o=o, b0=b0, dtB=dtB):
                        return e.tensor_tensor(out=dtB[:, o:o + TILE], in0=sc1[:, b0:b0 + 512],
                                               in1=sc1[:, b0 + 512:b0 + 1024], op=ALU.add)
                    tok_dt = V(v1, a5, dt_war[buf])
                    sp_war[ti % 2] = tok_dt
                    yield
                def ey(e, c=c, yB=yB):
                    return e.tensor_scalar(out=yB, in0=ln2c,
                                           scalar1=colp[:, CB["dvec"] + c:CB["dvec"] + c + 1],
                                           scalar2=None, op0=ALU.mult)
                tok_y = V(ey, ln_tok, tok_dt, y_war[buf])
                st[c] = (tok_dt, ln_tok, tok_y)
                yield

            g = prep_chunk(0)
            for _ in g:
                pass
            for c in range(DC):
                cbuf = c % 2
                dtB = dtB_[cbuf]
                yB = yB_[cbuf]
                tok_dt, ln_tok, tok_y = st[c]
                g = prep_chunk(c + 1) if c + 1 < DC else None
                def eu(e, dtB=dtB):
                    return e.tensor_tensor(out=uB, in0=dtB, in1=ln2c, op=ALU.mult)
                tok_u = V(eu, tok_dt, ln_tok)
                prev = None
                ta = None
                tt_ = None
                for si in range(DS):
                    buf = si % 2
                    AbB = AbB_[buf]; tokB = tokB_[buf]; hB = hB_[buf]; prB = prB_[buf]
                    bmslab = qT[:, buf * 2048:(buf + 1) * 2048]
                    cmslab = eT[:, buf * 2048:(buf + 1) * 2048]
                    tok_slab = None
                    tok_cslab = None
                    for q in range(4):
                        h2 = q % 2
                        def emb(e, si=si, q=q, h2=h2):
                            return e.matmul(out=pW0[:, h2 * 512:(h2 + 1) * 512],
                                            lhsT=selw(si),
                                            rhs=zbbf[64:96, q * 512:(q + 1) * 512],
                                            start=True, stop=True)
                        tb = T(emb, tok_xp, tok_sel, pw0_rd[h2])
                        def esb(e, q=q, h2=h2, bmslab=bmslab):
                            return e.activation(out=bmslab[:, q * 512:(q + 1) * 512],
                                                in_=pW0[:, h2 * 512:(h2 + 1) * 512],
                                                func=ACT.Identity)
                        tok_slab = S(esb, tb, slab_last[buf])
                        pw0_rd[h2] = tok_slab
                    for q in range(4):
                        h2 = q % 2
                        def emc(e, si=si, q=q, h2=h2):
                            return e.matmul(out=pW1[:, h2 * 512:(h2 + 1) * 512],
                                            lhsT=selw(16 + si),
                                            rhs=zbbf[64:96, q * 512:(q + 1) * 512],
                                            start=True, stop=True)
                        tc = T(emc, tok_xp, tok_sel, pw1_rd[h2])
                        def esc(e, q=q, h2=h2, cmslab=cmslab):
                            return e.activation(out=cmslab[:, q * 512:(q + 1) * 512],
                                                in_=pW1[:, h2 * 512:(h2 + 1) * 512],
                                                func=ACT.Identity)
                        tok_cslab = S(esc, tc, slab_last[buf])
                        pw1_rd[h2] = tok_cslab
                    def ea(e, c=c, si=si, AbB=AbB, dtB=dtB):
                        return e.activation(out=AbB, in_=dtB, func=ACT.Exp,
                                            scale=Acol[:, c * 16 + si:c * 16 + si + 1])
                    ta = S(ea, tok_dt, scan_done[buf])
                    def et(e, bmslab=bmslab, tokB=tokB):
                        return e.tensor_tensor(out=tokB, in0=uB, in1=bmslab, op=ALU.mult)
                    tt_ = V(et, tok_u, tok_slab, scan_done[buf])
                    def eh(e, AbB=AbB, tokB=tokB, hB=hB):
                        return e.tensor_tensor_scan(out=hB, data0=AbB, data1=tokB,
                                                    initial=zcol, op0=ALU.mult, op1=ALU.add)
                    th = V(eh, ta, tt_, tz, prod_done[buf])
                    scan_done[buf] = th
                    def ep(e, cmslab=cmslab, hB=hB, prB=prB):
                        return e.tensor_tensor(out=prB, in0=hB, in1=cmslab, op=ALU.mult)
                    tp = V(ep, th, yadd_done[buf], tok_cslab)
                    prod_done[buf] = tp
                    slab_last[buf] = tp
                    def ey2(e, prB=prB, yB=yB):
                        return e.tensor_tensor(out=yB, in0=yB, in1=prB, op=ALU.add)
                    prev = V(ey2, tp, tok_y, prev)
                    yadd_done[buf] = prev
                    if g is not None:
                        next(g, None)
                if g is not None:
                    for _ in g:
                        pass
                dt_war[cbuf] = ta
                txa = V(lambda e, c=c, yB=yB: e.tensor_tensor(out=xap(c), in0=xap(c), in1=yB,
                                                              op=ALU.add), prev)
                y_war[cbuf] = txa
            prog.barrier()

        if N_SUBLAYERS >= 4:
            ternary_sublayer(first=False)

        for c in range(DC):
            DMAO(lambda e, c=c, seq=seq: e.dma_start(
                out=out_h[seq, c], in_=xA[:, c * L:(c + 1) * L]).then_inc(SO, 16))
        prog.barrier()

    n_so = prog.cnt["so"]
    prog.plan["y"].append(([("so", n_so)], lambda e: e.nop(), None))

    with nc.Block() as block:
        def make(engname):
            def body(eng):
                wm = {}
                for deps, emit, inctok in prog.plan[engname]:
                    for (sn, val) in deps:
                        if val > wm.get(sn, 0):
                            eng.wait_ge(sems[sn], val)
                            wm[sn] = val
                    inst = emit(eng)
                    if inctok is not None:
                        inst.then_inc(sems[inctok[0]], 1)
            return body
        block.vector(make("v"))
        block.scalar(make("s"))
        block.tensor(make("t"))
        if prog.plan["g"]:
            block.gpsimd(make("g"))
        block.sync(make("y"))

    for h in reversed(ctx):
        h.__exit__(None, None, None)
    return nc


_NC_CACHE = {}


def kernel(**inputs):
    if "nc" not in _NC_CACHE:
        _NC_CACHE["nc"] = build_nc()
    nc = _NC_CACHE["nc"]

    x = np.asarray(inputs["L0_attn_a_x"])
    asc = np.ascontiguousarray
    f32 = np.float32
    shared = {
        "wq": asc(np.asarray(inputs["L0_attn_a_Wq"], dtype=f32)),
        "wk": asc(np.asarray(inputs["L0_attn_a_Wk"], dtype=f32)),
        "wv": asc(np.asarray(inputs["L0_attn_a_Wv"], dtype=f32)),
        "wo": asc(np.asarray(inputs["L0_attn_a_Wo"], dtype=f32)),
        "w1": asc(np.asarray(inputs["L0_attn_a_W1"], dtype=f32)),
        "w2": asc(np.asarray(inputs["L0_attn_a_W2"], dtype=f32)),
        "wt": asc(np.asarray(inputs["L0_ffn_weight"], dtype=f32).T),
        "wxp": asc(np.asarray(inputs["L0_attn_b_W_xproj"], dtype=f32).T
                   .reshape(8, 128, 96).transpose(1, 0, 2).reshape(128, 768)),
        "wdt": asc(np.asarray(inputs["L0_attn_b_W_dt"], dtype=f32).T),
        "alog": asc(np.asarray(inputs["L0_attn_b_A_log"], dtype=f32)
                    .reshape(8, 128, 16).transpose(1, 0, 2).reshape(128, 128)),
        "bq": asc(np.asarray(inputs["L0_attn_a_bq"], dtype=f32).reshape(8, 128).T),
        "bk": asc(np.asarray(inputs["L0_attn_a_bk"], dtype=f32)),
        "bv": asc(np.asarray(inputs["L0_attn_a_bv"], dtype=f32)),
        "bo": asc(np.asarray(inputs["L0_attn_a_bo"], dtype=f32).reshape(8, 128).T),
        "b1": asc(np.asarray(inputs["L0_attn_a_b1"], dtype=f32).reshape(32, 128).T),
        "b2": asc(np.asarray(inputs["L0_attn_a_b2"], dtype=f32).reshape(8, 128).T),
        "g1": asc(np.asarray(inputs["L0_attn_a_g1"], dtype=f32).reshape(8, 128).T),
        "be1": asc(np.asarray(inputs["L0_attn_a_beta1"], dtype=f32).reshape(8, 128).T),
        "g2": asc(np.asarray(inputs["L0_attn_a_g2"], dtype=f32).reshape(8, 128).T),
        "be2": asc(np.asarray(inputs["L0_attn_a_beta2"], dtype=f32).reshape(8, 128).T),
        "bdt": asc(np.asarray(inputs["L0_attn_b_b_dt"], dtype=f32).reshape(8, 128).T),
        "dvec": asc(np.asarray(inputs["L0_attn_b_D"], dtype=f32).reshape(8, 128).T),
        "biast": asc(np.asarray(inputs["L0_ffn_bias"], dtype=f32).reshape(8, 128).T),
        "alpha": asc(np.asarray(inputs["L0_ffn_alpha"], dtype=f32)),
    }
    in_maps = []
    for k in range(NCORES):
        xs = asc(x[2 * k:2 * k + 2].transpose(0, 2, 1)).reshape(NSEQ, DC, 128, L)
        m = dict(shared)
        m["x"] = xs
        in_maps.append(m)
    res = run_bass_kernel_spmd(nc, in_maps, core_ids=list(range(NCORES)))
    if DEBUG:
        global DBG_OUT
        DBG_OUT = {k: [np.asarray(res.results[i][k]) for i in range(NCORES)]
                   for k in res.results[0] if k.startswith("dbg_")}
    outs = []
    for k in range(NCORES):
        o = np.asarray(res.results[k]["o"]).reshape(NSEQ, D, L).transpose(0, 2, 1)
        outs.append(o)
    return np.concatenate(outs, axis=0).astype(np.float32)


# version 23
# speedup vs baseline: 1.1084x; 1.1084x over previous
import numpy as np
import concourse.bass as bass
import concourse.mybir as mybir
from concourse.bass_utils import run_bass_kernel_spmd

F32 = mybir.dt.float32
BF16 = mybir.dt.bfloat16
ALU = mybir.AluOpType
ACT = mybir.ActivationFunctionType
AX = mybir.AxisListType

NCORES = 8
NSEQ = 2
L = 2048
D = 1024
DC = 8
NH = 16
TILE = 512
NT = L // TILE
TT = 128
NTT = L // TT
FT = 256
NFT = L // FT
JG = 4
DS = 16
EPS_LN = 1e-5
EPS_AT = 1e-6

N_SUBLAYERS = 4
DEBUG = False
DBG_OUT = {}


class Prog:
    ENGS = ("v", "s", "t", "g", "y")

    def __init__(self):
        self.plan = {e: [] for e in self.ENGS}
        self.cnt = {"v": 0, "s": 0, "t": 0, "g": 0, "ld": 0, "ls": 0, "lg": 0, "so": 0}
        self.pending = {e: [] for e in self.ENGS}

    def step(self, eng, emit, deps=(), ndma=0, dmasem="ld"):
        flat = []
        for d in deps:
            if d is None:
                continue
            if isinstance(d, list):
                flat.extend(d)
            else:
                flat.append(d)
        deps = flat + self.pending[eng]
        self.pending[eng] = []
        if ndma:
            self.cnt[dmasem] += 16 * ndma
            tok = (dmasem, self.cnt[dmasem])
            self.plan[eng].append((deps, emit, None))
        else:
            self.cnt[eng] += 1
            tok = (eng, self.cnt[eng])
            self.plan[eng].append((deps, emit, tok))
        return tok

    def snap(self, sem):
        return (sem, self.cnt[sem])

    def barrier(self):
        toks = [(k, v) for k, v in self.cnt.items() if v > 0]
        for e in self.ENGS:
            self.pending[e] = list(toks)


def build_nc():
    nc = bass.Bass("TRN2", target_bir_lowering=False, dynamic_dma_scratch_size=256)

    def di(name, shape):
        return nc.dram_tensor(name, shape, F32, kind="ExternalInput").ap()
    x_h = di("x", [NSEQ, DC, 128, L])
    wq_h = di("wq", [D, D]); wk_h = di("wk", [D, D])
    wv_h = di("wv", [D, D]); wo_h = di("wo", [D, D])
    w1_h = di("w1", [D, 4 * D]); w2_h = di("w2", [4 * D, D])
    wt_h = di("wt", [D, D])
    wxp_h = di("wxp", [128, 768])
    wdt_h = di("wdt", [64, D])
    alog_h = di("alog", [128, 128])
    cnames = ["bq", "bo", "b2", "g1", "be1", "g2", "be2", "bdt", "dvec", "biast"]
    vecs = {n: di(n, [128, 8]) for n in cnames}
    vecs["bk"] = di("bk", [D]); vecs["bv"] = di("bv", [D])
    b1_h = di("b1", [128, 32])
    alpha_h = di("alpha", [1])
    out_h = nc.dram_tensor("o", [NSEQ, DC, 128, L], F32, kind="ExternalOutput").ap()
    dbg_h = {}

    ctx = []
    def sb(name, shape, dt=F32):
        h = nc.sbuf_tensor(name, shape, dt); t = h.__enter__(); ctx.append(h); return t
    def psum(name, shape, dt=F32):
        h = nc.psum_tensor(name, shape, dt); t = h.__enter__(); ctx.append(h); return t
    def semh(name):
        h = nc.semaphore(name); s = h.__enter__(); ctx.append(h); return s

    xA = sb("xA", [128, DC * L])
    xB = sb("xB", [128, DC * L])
    lnD = sb("lnD", [128, 6144], BF16)
    wW = sb("wW", [128, 2 * 8192], BF16)
    sc1 = sb("sc1", [128, 2048])
    sc2 = sb("sc2", [128, 1024])
    qT = sb("qT", [128, 4096], BF16)
    eT = sb("eT", [128, 4096], BF16)
    zb = sb("zb", [128, 1024])
    nmu_bc = sb("nmu_bc", [128, 1024])
    rsd_bc = sb("rsd_bc", [128, 1024])
    rows2 = sb("rows2", [128, 2048])
    colp = sb("colp", [128, 152])
    zxsel = sb("zxsel", [128, 128])
    Acol = sb("Acol", [128, 128])
    kvE = sb("kvE", [128, 512], BF16)
    ksE = sb("ksE", [128, 16], BF16)
    onesc = sb("onesc", [128, 1], BF16)
    onesr = sb("onesr", [1, 128], BF16)
    zrow = sb("zrow", [1, 128], BF16)
    CB = {"bq": 8, "negbq": 16, "bo": 24, "b2": 32, "g1": 40, "be1": 48,
          "g2": 56, "be2": 64, "dvec": 72, "bdt": 80, "biast": 88}

    pW0 = psum("pW0", [128, 1024])
    pW1 = psum("pW1", [128, 1024])
    pKV = psum("pKV", [128, 512])
    pKs = psum("pKs", [128, 8])
    pDen = psum("pDen", [2, 512])
    pSt = psum("pSt", [128, 512])

    sems = {n: semh(n) for n in ("v", "s", "t", "g", "ld", "ls", "lg", "so")}
    prog = Prog()

    def V(emit, *deps): return prog.step("v", emit, deps)
    def S(emit, *deps): return prog.step("s", emit, deps)
    def T(emit, *deps): return prog.step("t", emit, deps)
    def G(emit, *deps): return prog.step("g", emit, deps)
    def DMA(emit, *deps): return prog.step("y", emit, deps, ndma=1)
    def DMAS(emit, *deps): return prog.step("s", emit, deps, ndma=1, dmasem="ls")
    def DMAG(emit, *deps): return prog.step("g", emit, deps, ndma=1, dmasem="lg")
    def DMAO(emit, *deps): return prog.step("y", emit, deps, ndma=1, dmasem="so")
    LD = sems["ld"]; LS = sems["ls"]; SO = sems["so"]; LG = sems["lg"]

    def tap(name, ap, dt, *deps):
        if not DEBUG or name in dbg_h:
            return
        shape = list(ap.shape)
        dbg_h[name] = nc.dram_tensor("dbg_" + name, shape, dt, kind="ExternalOutput").ap()
        DMAO(lambda e, ap=ap, name=name: e.dma_start(out=dbg_h[name], in_=ap).then_inc(SO, 16),
             *deps)

    sc2bf = sc2[:, :].bitcast(BF16)
    qTf32 = qT[:, :].bitcast(F32)
    eTf32 = eT[:, :].bitcast(F32)
    zbbf = zb[:, :].bitcast(BF16)
    dtlow_bf = zbbf[0:64, 0:2048]

    def bcast(row_ap, p=128, n=L):
        return row_ap.unsqueeze(1).broadcast_to([row_ap.shape[0], p, n])

    def wslot(i):
        return wW[:, i * 8192:(i + 1) * 8192]

    def wchunk(slot, c, a, b):
        return wslot(slot)[:, c * 1024 + a: c * 1024 + b]

    wxpv = wslot(0)[:, 0:768]
    wdtv = wslot(0)[0:64, 1024:2048]

    def ld_col(nm, cc):
        DMA(lambda e, nm=nm, cc=cc: e.dma_start(
            out=colp[:, cc:cc + 8], in_=vecs[nm]).then_inc(LD, 16))
    for nm, cc in [("bq", 8), ("bo", 24), ("b2", 32), ("g1", 40), ("be1", 48),
                   ("g2", 56), ("be2", 64), ("dvec", 72), ("bdt", 80), ("biast", 88)]:
        ld_col(nm, cc)
    DMA(lambda e: e.dma_start(out=colp[:, 96:128], in_=b1_h).then_inc(LD, 16))
    DMA(lambda e: e.dma_start(
        out=colp[:, 128:129],
        in_=bcast(alpha_h.rearrange("(a b) -> a b", a=1), 128, 1)).then_inc(LD, 16))
    DMA(lambda e: e.dma_start(out=Acol[:, :], in_=alog_h).then_inc(LD, 16))
    tok_ld0 = prog.snap("ld")

    V(lambda e: e.memset(onesc[:], 1.0))
    V(lambda e: e.memset(colp[:, 131:132], 1.0))
    V(lambda e: e.memset(onesr[:], 1.0))
    V(lambda e: e.memset(zrow[:], 0.0))
    V(lambda e: e.memset(colp[:, 1:2], EPS_LN))
    V(lambda e: e.memset(colp[:, 3:4], EPS_AT))
    V(lambda e: e.memset(colp[:, 129:130], 1.0))
    V(lambda e: e.tensor_scalar(out=colp[:, 16:24], in0=colp[:, 8:16],
                                scalar1=-1.0, scalar2=None, op0=ALU.mult), tok_ld0)
    t_ea = S(lambda e: e.activation(out=sc1[:, 0:128], in_=Acol[:, :], func=ACT.Exp), tok_ld0)
    V(lambda e: e.tensor_scalar(out=Acol[:, :], in0=sc1[:, 0:128],
                                scalar1=-1.0, scalar2=None, op0=ALU.mult), t_ea)
    prog.barrier()

    sc1_war = [None, None]
    wcast = {0: [None] * 8, 1: [None] * 8}

    def load_weight_1024(slot, hbm, rows0=0, cols0=0, dep0=None):
        war = prog.snap("t")
        last = None
        half = [sc1_war[0], sc1_war[1]]
        for c in range(8):
            sl = c % 2
            deps = [half[sl], dep0 if c < 2 else None]
            if sl == 0:
                def em(e, c=c, sl=sl, hbm=hbm, rows0=rows0, cols0=cols0):
                    return e.dma_start(out=sc1[:, sl * 1024:(sl + 1) * 1024],
                                       in_=hbm[rows0 + c * 128: rows0 + (c + 1) * 128,
                                               cols0:cols0 + 1024]).then_inc(LD, 16)
                DMA(em, *deps)
                tdma = prog.snap("ld")
            else:
                def em(e, c=c, sl=sl, hbm=hbm, rows0=rows0, cols0=cols0):
                    return e.dma_start(out=sc1[:, sl * 1024:(sl + 1) * 1024],
                                       in_=hbm[rows0 + c * 128: rows0 + (c + 1) * 128,
                                               cols0:cols0 + 1024]).then_inc(LS, 16)
                DMAS(em, *deps)
                tdma = prog.snap("ls")
            def ec(e, c=c, sl=sl, slot=slot):
                return e.activation(out=wchunk(slot, c, 0, 1024),
                                    in_=sc1[:, sl * 1024:(sl + 1) * 1024], func=ACT.Identity)
            tcast = S(ec, tdma, war if c == 0 else None)
            half[sl] = tcast
            last = tcast
            wcast[slot][c] = tcast
        sc1_war[0] = half[0]
        sc1_war[1] = half[1]
        return last

    pst_war = [None]

    def ln_stats(src, alpha_scale=False, wdeps=()):
        sxr = qTf32[0:1, 0:2048]
        sqr = eTf32[0:1, 0:2048]
        m2r = sc1[0:1, 0:2048]
        last_mm = None
        last_ev = None
        stg_war = [None] * DC
        for piece in range(8):
            o = piece * 256
            sq_toks = {}
            for c in range(DC):
                def esq(e, o=o, c=c):
                    return e.activation(out=sc2bf[:, c * 256:(c + 1) * 256],
                                        in_=src(c)[:, o:o + 256], func=ACT.Square)
                deps = list(wdeps) if (piece == 0 and c == 0) else []
                deps.append(stg_war[c])
                sq_toks[c] = S(esq, *deps)
            for c in range(DC):
                def emx(e, c=c, o=o):
                    return e.matmul(out=pSt[0:1, 0:256], lhsT=colp[:, 131:132],
                                    rhs=src(c)[:, o:o + 256],
                                    start=(c == 0), stop=(c == DC - 1))
                depsx = list(wdeps) if (piece == 0 and c == 0) else []
                if c == 0:
                    depsx.append(last_ev)
                    if piece == 0:
                        depsx.append(pst_war[0])
                tmx = T(emx, *depsx)
            for c in range(DC):
                def emq(e, c=c):
                    return e.matmul(out=pSt[0:1, 256:512], lhsT=onesc[:, :],
                                    rhs=sc2bf[:, c * 256:(c + 1) * 256],
                                    start=(c == 0), stop=(c == DC - 1))
                last_mm = T(emq, sq_toks[c])
                stg_war[c] = last_mm
            def ev1(e, o=o):
                return e.tensor_copy(out=sxr[:, o:o + 256], in_=pSt[0:1, 0:256])
            def ev2(e, o=o):
                return e.tensor_copy(out=sqr[:, o:o + 256], in_=pSt[0:1, 256:512])
            t1 = V(ev1, last_mm)
            last_ev = V(ev2, t1)
            pst_war[0] = last_ev
        tnm = V(lambda e: e.tensor_scalar(out=sxr, in0=sxr,
                                          scalar1=-1.0 / D, scalar2=None, op0=ALU.mult), last_ev)
        tm2 = S(lambda e: e.activation(out=m2r, in_=sxr, func=ACT.Square), tnm)
        tms = V(lambda e: e.tensor_scalar(out=sqr, in0=sqr,
                                          scalar1=1.0 / D, scalar2=None, op0=ALU.mult), last_ev)
        tvr = V(lambda e: e.tensor_tensor(out=sqr, in0=sqr, in1=m2r, op=ALU.subtract), tms, tm2)
        tln = S(lambda e: e.activation(out=sqr, in_=sqr, func=ACT.Ln,
                                       bias=colp[0:1, 1:2]), tvr)
        trs = S(lambda e: e.activation(out=sqr, in_=sqr, func=ACT.Exp, scale=-0.5), tln)
        tfin = trs
        if alpha_scale:
            tfin = V(lambda e: e.tensor_scalar(out=sqr, in0=sqr,
                                               scalar1=colp[0:1, 128:129], scalar2=None,
                                               op0=ALU.mult), trs)
        DMA(lambda e: e.dma_start(out=rows2[32:33, :], in_=sxr).then_inc(LD, 16), tnm, tfin)
        DMA(lambda e: e.dma_start(out=rows2[64:65, :], in_=sqr).then_inc(LD, 16), tfin)
        tok = prog.snap("ld")
        return (tok, tok)

    class BC:
        def __init__(self, rows_tok, global_war):
            self.rows_tok = rows_tok
            self.war = {0: global_war, 1: global_war}
            self.tok = {}

        def load(self, ti):
            half = ti % 2
            o = (ti * 512) % L
            def m1(e, o=o):
                return e.matmul(out=pSt[:, :], lhsT=zxsel[32:33, 0:128],
                                rhs=rows2[32:33, o:o + 512], start=True, stop=True)
            t1 = T(m1, self.rows_tok[0], self.rows_tok[1], pst_war[0], tok_ones)
            def s1(e, half=half):
                return e.activation(out=nmu_bc[:, half * 512:(half + 1) * 512],
                                    in_=pSt[:, :], func=ACT.Identity)
            ts1 = S(s1, t1, self.war[half])
            def m2(e, o=o):
                return e.matmul(out=pSt[:, :], lhsT=zxsel[64:65, 0:128],
                                rhs=rows2[64:65, o:o + 512], start=True, stop=True)
            t2 = T(m2, ts1)
            def s2(e, half=half):
                return e.activation(out=rsd_bc[:, half * 512:(half + 1) * 512],
                                    in_=pSt[:, :], func=ACT.Identity)
            ts2 = S(s2, t2, self.war[half])
            pst_war[0] = ts2
            self.tok[ti] = [ts1, ts2]
            return self.tok[ti]

        def nmu(self, ti, a=0, b=512):
            half = ti % 2
            return nmu_bc[:, half * 512 + a: half * 512 + b]

        def rsd(self, ti, a=0, b=512):
            half = ti % 2
            return rsd_bc[:, half * 512 + a: half * 512 + b]

        def done(self, ti, tok):
            self.war[ti % 2] = tok

    def pclear(region_ap, ncols, *deps):
        t = None
        for k in range(0, ncols, 128):
            w = min(128, ncols - k)
            def em(e, k=k, w=w, region_ap=region_ap):
                return e.matmul(out=region_ap[:, k:k + w], lhsT=zrow[0:1, 0:128],
                                rhs=onesr[0:1, 0:w], start=True, stop=False)
            t = T(em, *deps)
        return t

    def load_kv_bias_rows(dep0):
        DMA(lambda e: e.dma_start(out=sc2[0:1, 0:1024],
                                  in_=vecs["bk"].rearrange("(a b) -> a b", a=1)).then_inc(LD, 16),
            dep0)
        t1 = V(lambda e: e.tensor_copy(out=zbbf[0:1, 0:1024], in_=sc2[0:1, 0:1024]),
               prog.snap("ld"))
        DMA(lambda e: e.dma_start(out=sc2[0:1, 0:1024],
                                  in_=vecs["bv"].rearrange("(a b) -> a b", a=1)).then_inc(LD, 16),
            t1)
        t2 = V(lambda e: e.tensor_copy(out=zbbf[0:1, 1024:2048], in_=sc2[0:1, 0:1024]),
               prog.snap("ld"))
        return t2

    xap = lambda c: xA[:, c * L:(c + 1) * L]
    xbap = lambda c: xB[:, c * L:(c + 1) * L]

    def xa2bf(c, a, b):
        if c < 2:
            return qT[:, c * 2048 + a: c * 2048 + b]
        if c < 4:
            return eT[:, (c - 2) * 2048 + a: (c - 2) * 2048 + b]
        if c == 4:
            return zbbf[:, a:b]
        if c < 7:
            return lnD[:, 2048 + (c - 5) * 2048 + a: 2048 + (c - 5) * 2048 + b]
        return sc2bf[:, a:b]

    tzx1 = V(lambda e: e.memset(zxsel[0:1, 0:64], 1.0))
    tzx2 = V(lambda e: e.memset(zxsel[0:1, 64:128], 0.0), tzx1)
    V(lambda e: e.memset(zxsel[32:33, 0:128], 1.0))
    V(lambda e: e.memset(zxsel[64:65, 0:128], 1.0))
    tok_ones = prog.snap("v")
    DMA(lambda e: e.dma_start(out=zxsel[1:2, 64:128],
                              in_=zxsel[0:1, 0:64]).then_inc(LD, 16), tzx2)
    DMA(lambda e: e.dma_start(out=zxsel[1:2, 0:64],
                              in_=zxsel[0:1, 64:128]).then_inc(LD, 16))
    tok_zxsel = prog.snap("ld")

    for seq in range(NSEQ):
        for c in range(DC):
            DMA(lambda e, c=c, seq=seq: e.dma_start(
                out=xA[:, c * L:(c + 1) * L], in_=x_h[seq, c]).then_inc(LD, 16))
        tok_x = prog.snap("ld")

        rt0 = ln_stats(xap, wdeps=[tok_x])
        tok_bkv = load_kv_bias_rows(rt0[1])
        tap("rows2", rows2[:, :], F32, rt0[0], rt0[1])
        tok_wk = load_weight_1024(0, wk_h, dep0=rt0[1])
        tok_wv = load_weight_1024(1, wv_h)
        tok_ckv = pclear(pKV[:, :], 512)
        tok_cks = pclear(pKs[:, :], 8)
        bc0 = BC(rt0, None)

        kv_mm_tok = None
        last_kevac = None
        last_vevac = None
        xn_war = {}
        for tt in range(NTT):
            o = tt * TT
            ti = tt // 4
            if tt % 4 == 0:
                bc0.load(ti)
            tok_bc = bc0.tok[ti]
            xn_tt = eT[:, 2048 + (tt % 2) * 1024:3072 + (tt % 2) * 1024]
            t_xn = None
            for c in range(DC):
                oin = (tt % 4) * 128
                def e1(e, c=c, o=o, ti=ti, oin=oin):
                    return e.tensor_tensor(out=sc2[:, 0:128], in0=xap(c)[:, o:o + TT],
                                           in1=bc0.nmu(ti, oin, oin + 128), op=ALU.add)
                deps = [tok_bc]
                if c == 0:
                    deps.append(xn_war.get(tt % 2))
                d1 = V(e1, *deps)
                def e2(e, c=c, ti=ti, oin=oin, xn_tt=xn_tt):
                    return e.tensor_tensor(out=xn_tt[:, c * 128:(c + 1) * 128], in0=sc2[:, 0:128],
                                           in1=bc0.rsd(ti, oin, oin + 128), op=ALU.mult)
                t_xn = V(e2, d1)
            bc0.done(ti, t_xn)
            tok_xn = t_xn
            if tt == 0:
                tap("xn_tt0", eT[:, 2048:3072], BF16, tok_xn)
            mm_last = None
            for proj, slot, pdst, wtok in (("k", 0, pW0, tok_wk), ("v", 1, pW1, tok_wv)):
                for h in range(2):
                    for c in range(DC):
                        def em(e, c=c, h=h, slot=slot, pdst=pdst, xn_tt=xn_tt):
                            return e.matmul(out=pdst[:, h * 512:(h + 1) * 512],
                                            lhsT=xn_tt[:, c * 128:(c + 1) * 128],
                                            rhs=wchunk(slot, c, h * 512, (h + 1) * 512),
                                            start=(c == 0), stop=False)
                        deps = [wcast[slot][c]]
                        if c == 0:
                            deps += [tok_xn,
                                     last_kevac if slot == 0 else last_vevac]
                        mm_last = T(em, *deps)
                    def eb(e, h=h, proj=proj, pdst=pdst):
                        off = 0 if proj == "k" else 1024
                        return e.matmul(out=pdst[:, h * 512:(h + 1) * 512],
                                        lhsT=onesr[0:1, 0:128],
                                        rhs=zbbf[0:1, off + h * 512: off + (h + 1) * 512],
                                        start=False, stop=True)
                    deps = [tok_bkv] if tt == 0 else []
                    mm_last = T(eb, *deps)
            tok_kvproj = mm_last
            k_tm = eT[:, 0:1024]
            v_tm = eT[:, 1024:2048]
            ts1 = S(lambda e: e.activation(out=qT[:, 0:1024], in_=pW0[:, :], func=ACT.Relu),
                    tok_kvproj)
            ts3 = None
            for h2 in range(2):
                def es2(e, h2=h2):
                    return e.activation(out=sc2[:, 512:1024],
                                        in_=pW0[:, h2 * 512:(h2 + 1) * 512],
                                        func=ACT.Relu, scale=-1.0)
                ts2 = S(es2, tok_kvproj if h2 == 0 else ts3)
                def es3(e, h2=h2):
                    return e.activation(out=qT[:, 1024 + h2 * 512:1536 + h2 * 512],
                                        in_=sc2[:, 512:1024], func=ACT.Exp, scale=-1.0)
                ts3 = S(es3, ts2)
            tk = V(lambda e: e.tensor_tensor(out=k_tm[:, :], in0=qT[:, 0:1024],
                                             in1=qT[:, 1024:2048], op=ALU.add),
                   ts1, ts3, kv_mm_tok)
            last_kevac = tk
            tv = S(lambda e: e.activation(out=v_tm[:, :], in_=pW1[:, :], func=ACT.Identity),
                   tok_kvproj, kv_mm_tok)
            last_vevac = tv
            if tt == 0:
                tap("kv_tt0", eT[:, 0:2048], BF16, tk, tv)
            for hh in range(NH):
                def ekv(e, hh=hh):
                    return e.matmul(out=pKV[64 * (hh % 2):64 * (hh % 2) + 64,
                                          (hh // 2) * 64:(hh // 2) * 64 + 64],
                                    lhsT=k_tm[:, hh * 64:(hh + 1) * 64],
                                    rhs=v_tm[:, hh * 64:(hh + 1) * 64],
                                    start=False, stop=(tt == NTT - 1))
                deps = [tk, tv] if hh == 0 else []
                if hh == 0 and tt == 0:
                    deps += [tok_ckv, tok_cks]
                T(ekv, *deps)
                def eks(e, hh=hh):
                    return e.matmul(out=pKs[64 * (hh % 2):64 * (hh % 2) + 64,
                                          (hh // 2):(hh // 2) + 1],
                                    lhsT=k_tm[:, hh * 64:(hh + 1) * 64],
                                    rhs=onesc[:, :],
                                    start=False, stop=(tt == NTT - 1))
                kv_mm_tok = T(eks)
            xn_war[tt % 2] = kv_mm_tok
        tkve = V(lambda e: e.tensor_copy(out=kvE[:, :], in_=pKV[:, :]), kv_mm_tok)
        tks0 = V(lambda e: e.memset(ksE[:, :], 0.0))
        tok_kv = tkve
        for hh in range(NH):
            def ek(e, hh=hh):
                return e.tensor_copy(out=ksE[64 * (hh % 2):64 * (hh % 2) + 64, hh:hh + 1],
                                     in_=pKs[64 * (hh % 2):64 * (hh % 2) + 64,
                                             (hh // 2):(hh // 2) + 1])
            tok_kv = V(ek, kv_mm_tok, tks0)
        tap("kvE", kvE[:, :], BF16, tok_kv)
        tap("ksE", ksE[:, :], BF16, tok_kv)
        prog.barrier()

        tok_wq = load_weight_1024(0, wq_h)
        tok_wo = load_weight_1024(1, wo_h)

        zmm_rd = None
        bc2 = BC(rt0, tok_xn)
        last_wo_ev = None
        for ti in range(NT):
            o = ti * TILE
            bc2.load(ti)
            tok_bc = bc2.tok[ti]
            xn_bf = eT
            t_xn = None
            for c in range(DC):
                def e1(e, c=c, o=o, ti=ti):
                    return e.tensor_tensor(out=sc2[:, 0:512], in0=xap(c)[:, o:o + TILE],
                                           in1=bc2.nmu(ti), op=ALU.add)
                deps = [tok_bc]
                if ti > 0 and c == 0:
                    deps.append(last_wo_ev)
                d1 = V(e1, *deps)
                def e2(e, c=c, ti=ti):
                    return e.tensor_tensor(out=xn_bf[:, c * 512:(c + 1) * 512], in0=sc2[:, 0:512],
                                           in1=bc2.rsd(ti), op=ALU.mult)
                t_xn = V(e2, d1)
            bc2.done(ti, t_xn)
            tok_xn = t_xn
            q_rd = {}
            tok_q = None
            for m in range(DC):
                half = m % 2
                for c in range(DC):
                    def em(e, m=m, c=c, half=half):
                        return e.matmul(out=pW0[:, half * 512:(half + 1) * 512],
                                        lhsT=wchunk(0, c, m * 128, (m + 1) * 128),
                                        rhs=xn_bf[:, c * 512:(c + 1) * 512],
                                        start=(c == 0), stop=(c == DC - 1))
                    deps = [wcast[0][c]]
                    if c == 0:
                        deps.append(tok_xn)
                        if m >= 2:
                            deps.append(q_rd[m - 2])
                    tq = T(em, *deps)
                def s1(e, half=half, m=m):
                    return e.activation(out=sc2bf[:, 0:512],
                                        in_=pW0[:, half * 512:(half + 1) * 512],
                                        func=ACT.Relu, bias=colp[:, CB["bq"] + m:CB["bq"] + m + 1])
                a1 = S(s1, tq, tok_q)
                def s2(e, half=half, m=m):
                    return e.activation(out=sc2[:, 256:768],
                                        in_=pW0[:, half * 512:(half + 1) * 512],
                                        func=ACT.Relu, scale=-1.0,
                                        bias=colp[:, CB["negbq"] + m:CB["negbq"] + m + 1])
                a2 = S(s2, a1)
                a3 = S(lambda e: e.activation(out=sc2bf[:, 1536:2048], in_=sc2[:, 256:768],
                                              func=ACT.Exp, scale=-1.0), a2)
                def v1(e, m=m):
                    return e.tensor_tensor(out=qT[:, m * 512:(m + 1) * 512],
                                           in0=sc2bf[:, 0:512], in1=sc2bf[:, 1536:2048],
                                           op=ALU.add)
                tok_q = V(v1, a3)
                q_rd[m] = a2
            if ti == 0:
                tap("q_t0", qT[:, :], BF16, tok_q)
            den_ev = {}
            zb_tok = {}
            zbd_tok = {}
            o_ev = []
            for c in range(DC):
                half = c % 2
                dhalf = c % 2
                def ed(e, c=c):
                    return e.matmul(out=pDen[:, :], lhsT=ksE[:, 2 * c:2 * c + 2],
                                    rhs=qT[:, c * 512:(c + 1) * 512], start=True, stop=True)
                deps = [tok_q, tok_kv]
                if c >= 1:
                    deps.append(den_ev[c - 1])
                td = T(ed, *deps)
                def ev(e, dhalf=dhalf):
                    return e.tensor_copy(out=sc2[0:2, dhalf * 512:(dhalf + 1) * 512],
                                         in_=pDen[:, :])
                deps2 = [td]
                if c >= 2:
                    deps2.append(zb_tok[c - 2])
                    deps2.append(zbd_tok[c - 2])
                tde = V(ev, *deps2)
                den_ev[c] = tde
                def ez1(e, dhalf=dhalf):
                    return e.activation(out=sc2[0:2, dhalf * 512:(dhalf + 1) * 512],
                                        in_=sc2[0:2, dhalf * 512:(dhalf + 1) * 512],
                                        func=ACT.Ln, bias=colp[0:2, 3:4])
                tz1 = S(ez1, tde)
                def ez2(e, dhalf=dhalf):
                    return e.activation(out=sc2[0:2, dhalf * 512:(dhalf + 1) * 512],
                                        in_=sc2[0:2, dhalf * 512:(dhalf + 1) * 512],
                                        func=ACT.Exp, scale=-1.0)
                tz2 = S(ez2, tz1)
                tmo = None
                for hh in (2 * c, 2 * c + 1):
                    def eo(e, hh=hh, c=c, half=half):
                        return e.matmul(out=pW1[64 * (hh % 2):64 * (hh % 2) + 64,
                                              half * 512:(half + 1) * 512],
                                        lhsT=kvE[64 * (hh % 2):64 * (hh % 2) + 64,
                                                 (hh // 2) * 64:(hh // 2) * 64 + 64],
                                        rhs=qT[64 * (hh % 2):64 * (hh % 2) + 64,
                                               c * 512:(c + 1) * 512],
                                        start=True, stop=True)
                    deps = [tok_kv, tok_q]
                    if len(o_ev) >= 2:
                        deps.append(o_ev[-2])
                    tmo = T(eo, *deps)
                def ezm(e, dhalf=dhalf):
                    return e.matmul(out=pKV[:, 0:512], lhsT=zxsel[0:2, 0:128],
                                    rhs=sc2[0:2, dhalf * 512:(dhalf + 1) * 512],
                                    start=True, stop=True)
                t_zmm = T(ezm, tz2, tok_zxsel, tok_kv, zmm_rd)
                zb_tok[c] = t_zmm
                zbd_tok[c] = t_zmm
                def ezev(e, half=half):
                    return e.activation(out=zb[:, half * 512:(half + 1) * 512],
                                        in_=pKV[:, 0:512], func=ACT.Identity)
                t_zev = S(ezev, t_zmm, o_ev[-2] if len(o_ev) >= 2 else None)
                zmm_rd = t_zev
                def evv(e, c=c, half=half):
                    return e.tensor_tensor(out=qT[:, c * 512:(c + 1) * 512],
                                           in0=pW1[:, half * 512:(half + 1) * 512],
                                           in1=zb[:, half * 512:(half + 1) * 512], op=ALU.mult)
                o_ev.append(V(evv, tmo, t_zev))
            tok_osc = o_ev[-1]
            if ti == 0:
                tap("osc_t0", qT[:, :], BF16, tok_osc)
            wo_rd = {}
            for m in range(DC):
                half = m % 2
                for c in range(DC):
                    def em(e, m=m, c=c, half=half):
                        return e.matmul(out=pW0[:, half * 512:(half + 1) * 512],
                                        lhsT=wchunk(1, c, m * 128, (m + 1) * 128),
                                        rhs=qT[:, c * 512:(c + 1) * 512],
                                        start=(c == 0), stop=(c == DC - 1))
                    deps = [wcast[1][c]]
                    if c == 0:
                        deps.append(tok_osc)
                        deps.append(wo_rd[m - 2] if m >= 2 else q_rd[6 + m])
                    tw = T(em, *deps)
                def s1(e, m=m, half=half):
                    return e.activation(out=sc2[:, 0:512], in_=pW0[:, half * 512:(half + 1) * 512],
                                        func=ACT.Identity,
                                        bias=colp[:, CB["bo"] + m:CB["bo"] + m + 1])
                aw = S(s1, tw, last_wo_ev)
                def v1(e, m=m, o=o):
                    return e.tensor_tensor(out=xbap(m)[:, o:o + TILE], in0=sc2[:, 0:512],
                                           in1=xn_bf[:, m * 512:(m + 1) * 512], op=ALU.add)
                last_wo_ev = V(v1, aw)
                wo_rd[m] = aw
        tap("xa_c0", xbap(0), F32, last_wo_ev)
        prog.barrier()

        rt1 = ln_stats(xbap)
        bc3 = BC(rt1, None)
        tok_w1 = load_weight_1024(0, w1_h, rows0=0, cols0=0, dep0=rt1[1])
        tok_w2 = load_weight_1024(1, w2_h, rows0=0, cols0=0, dep0=rt1[1])
        tok_xa2bf = None
        for ti in range(NT):
            o = ti * TILE
            bc3.load(ti)
            tok_bc = bc3.tok[ti]
            tlast = None
            for c in range(DC):
                d1 = V(lambda e, c=c, o=o, ti=ti: e.tensor_tensor(
                    out=xbap(c)[:, o:o + TILE], in0=xbap(c)[:, o:o + TILE],
                    in1=bc3.nmu(ti), op=ALU.add), tok_bc)
                d2 = V(lambda e, c=c, o=o, ti=ti: e.tensor_tensor(
                    out=xbap(c)[:, o:o + TILE], in0=xbap(c)[:, o:o + TILE],
                    in1=bc3.rsd(ti), op=ALU.mult), d1)
                d3 = V(lambda e, c=c, o=o: e.tensor_scalar(
                    out=xbap(c)[:, o:o + TILE], in0=xbap(c)[:, o:o + TILE],
                    scalar1=colp[:, CB["g1"] + c:CB["g1"] + c + 1],
                    scalar2=None, op0=ALU.mult), d2)
                d4 = V(lambda e, c=c, o=o: e.tensor_scalar(
                    out=xbap(c)[:, o:o + TILE], in0=xbap(c)[:, o:o + TILE],
                    scalar1=colp[:, CB["be1"] + c:CB["be1"] + c + 1],
                    scalar2=None, op0=ALU.add), d3)
                tlast = V(lambda e, c=c, o=o: e.tensor_copy(
                    out=xa2bf(c, o, o + TILE), in_=xbap(c)[:, o:o + TILE]), d4)
            bc3.done(ti, tlast)
            tok_xa2bf = tlast
        tap("xa2_c0", xbap(0), F32, tok_xa2bf)
        tok_xa2_done = tok_xa2bf
        tap("xa2bf0", qT[:, 0:2048], BF16, tok_xa2bf)
        tap("xa2bf4", zbbf[:, 0:2048], BF16, tok_xa2bf)
        tap("xa2bf5", lnD[:, 2048:4096], BF16, tok_xa2bf)
        tap("xa2bf7", sc2bf[:, 0:2048], BF16, tok_xa2bf)

        last_w2ev = None
        eT_war = None
        for jg in range(JG):
            if jg > 0:
                tok_w2 = load_weight_1024(1, w2_h, rows0=jg * 1024, cols0=0)
            for ti in range(NFT):
                o = ti * FT
                h_toks = {}
                tok_h1 = None
                for j in range(8):
                    half = j % 2
                    for c in range(DC):
                        def em(e, j=j, c=c, half=half, o=o):
                            return e.matmul(out=pW0[:, half * 512:half * 512 + FT],
                                            lhsT=wchunk(0, c, j * 128, (j + 1) * 128),
                                            rhs=xa2bf(c, o, o + FT),
                                            start=(c == 0), stop=(c == DC - 1))
                        deps = [wcast[0][c]]
                        if c == 0:
                            deps.append(tok_xa2bf)
                            if j >= 2:
                                deps.append(h_toks[j - 2])
                            elif last_w2ev is not None:
                                deps.append(last_w2ev)
                        T(em, *deps)
                    jglob = jg * 8 + j
                    def s1(e, j=j, half=half, jglob=jglob):
                        return e.activation(out=lnD[:, j * FT:(j + 1) * FT],
                                            in_=pW0[:, half * 512:half * 512 + FT],
                                            func=ACT.Relu,
                                            bias=colp[:, 96 + jglob:96 + jglob + 1])
                    tok_h1 = S(s1, prog.snap("t"))
                    h_toks[j] = tok_h1
                if ti == NFT - 1 and jg + 1 < JG:
                    load_weight_1024(0, w1_h, rows0=0, cols0=(jg + 1) * 1024)
                if jg == 0 and ti == 0:
                    tap("h1_00", lnD[:, 0:2048], BF16, tok_h1)
                for m in range(DC):
                    half = m % 2
                    for j in range(8):
                        def em(e, m=m, j=j, half=half):
                            return e.matmul(out=pW1[:, half * 512:half * 512 + FT],
                                            lhsT=wchunk(1, j, m * 128, (m + 1) * 128),
                                            rhs=lnD[:, j * FT:(j + 1) * FT],
                                            start=(j == 0), stop=(j == 7))
                        deps = [wcast[1][j]]
                        if j == 0:
                            deps.append(tok_h1)
                            if last_w2ev is not None and m >= 2:
                                deps.append(last_w2ev)
                        tw = T(em, *deps)
                    if jg == 0:
                        def sb2(e, m=m, half=half):
                            return e.activation(out=nmu_bc[:, 0:FT],
                                                in_=pW1[:, half * 512:half * 512 + FT],
                                                func=ACT.Identity,
                                                bias=colp[:, CB["b2"] + m:CB["b2"] + m + 1])
                        tsb = S(sb2, tw, last_w2ev)
                        def ev(e, m=m, o=o):
                            return e.tensor_tensor(out=xbap(m)[:, o:o + FT],
                                                   in0=nmu_bc[:, 0:FT],
                                                   in1=xbap(m)[:, o:o + FT], op=ALU.add)
                        last_w2ev = V(ev, tsb)
                    else:
                        def ev(e, m=m, o=o, half=half):
                            return e.tensor_tensor(out=xbap(m)[:, o:o + FT],
                                                   in0=pW1[:, half * 512:half * 512 + FT],
                                                   in1=xbap(m)[:, o:o + FT], op=ALU.add)
                        last_w2ev = V(ev, tw)
            tap("ffn_jg%d" % jg, xbap(0), F32, last_w2ev)
        tap("x2py_c0", xbap(0), F32, last_w2ev)
        prog.barrier()

        rt2 = ln_stats(xbap)
        bc4 = BC(rt2, None)
        for ti in range(NT):
            o = ti * TILE
            bc4.load(ti)
            tok_bc = bc4.tok[ti]
            tlast = None
            for c in range(DC):
                d1 = V(lambda e, c=c, o=o, ti=ti: e.tensor_tensor(
                    out=sc1[:, 0:512], in0=xbap(c)[:, o:o + TILE], in1=bc4.nmu(ti), op=ALU.add),
                    tok_bc)
                d2 = V(lambda e, ti=ti: e.tensor_tensor(
                    out=sc1[:, 0:512], in0=sc1[:, 0:512], in1=bc4.rsd(ti), op=ALU.mult), d1)
                d3 = V(lambda e, c=c: e.tensor_scalar(
                    out=sc1[:, 0:512], in0=sc1[:, 0:512],
                    scalar1=colp[:, CB["g2"] + c:CB["g2"] + c + 1],
                    scalar2=None, op0=ALU.mult), d2)
                d4 = V(lambda e, c=c: e.tensor_scalar(
                    out=sc1[:, 0:512], in0=sc1[:, 0:512],
                    scalar1=colp[:, CB["be2"] + c:CB["be2"] + c + 1],
                    scalar2=None, op0=ALU.add), d3)
                tlast = V(lambda e, c=c, o=o: e.tensor_tensor(
                    out=xap(c)[:, o:o + TILE], in0=xap(c)[:, o:o + TILE], in1=sc1[:, 0:512],
                    op=ALU.add), d4)
            bc4.done(ti, tlast)
        prog.barrier()

        def ternary_prep(rt):
            rtoks = []
            for c in range(DC):
                sl = c % 2
                deps = [rtoks[c - 2]] if c >= 2 else [rt[1], sc1_war[sl]]
                if sl == 0:
                    def em(e, c=c, sl=sl):
                        return e.dma_start(out=sc1[:, sl * 1024:(sl + 1) * 1024],
                                           in_=wt_h[c * 128:(c + 1) * 128, :]).then_inc(LD, 16)
                    DMA(em, *deps)
                    td = prog.snap("ld")
                else:
                    def em(e, c=c, sl=sl):
                        return e.dma_start(out=sc1[:, sl * 1024:(sl + 1) * 1024],
                                           in_=wt_h[c * 128:(c + 1) * 128, :]).then_inc(LS, 16)
                    DMAS(em, *deps)
                    td = prog.snap("ls")
                def er(e, c=c, sl=sl):
                    return e.tensor_reduce(out=colp[:, 136 + c:137 + c],
                                           in_=sc1[:, sl * 1024:(sl + 1) * 1024],
                                           axis=AX.X, op=ALU.add, apply_absolute_value=True)
                rtoks.append(V(er, td))
            tsum = V(lambda e: e.tensor_reduce(out=colp[:, 144:145], in_=colp[:, 136:144],
                                               axis=AX.X, op=ALU.add), rtoks[-1])
            tap("tern_redc", colp[:, 130:146], F32, tsum)
            tmm = T(lambda e: e.matmul(out=pDen[0:1, 0:1], lhsT=colp[:, 131:132],
                                       rhs=colp[:, 144:145],
                                       start=True, stop=True), tsum)
            tev = V(lambda e: e.tensor_scalar(out=colp[0:1, 135:136], in0=pDen[0:1, 0:1],
                                              scalar1=0.7 / (1024.0 * 1024.0), scalar2=None,
                                              op0=ALU.mult), tmm)
            DMA(lambda e: e.dma_start(out=colp[1:2, 135:136],
                                      in_=colp[0:1, 135:136]).then_inc(LD, 16), tev)
            tok_dd = prog.snap("ld")
            tdm = T(lambda e: e.matmul(out=pSt[:, 0:1], lhsT=zxsel[0:2, 0:128],
                                       rhs=colp[0:2, 135:136], start=True, stop=True),
                    tok_dd, pst_war[0], tok_zxsel)
            tok_d = S(lambda e: e.activation(out=colp[:, 130:131], in_=pSt[:, 0:1],
                                             func=ACT.Identity), tdm)
            pst_war[0] = tok_d
            last = None
            qtoks = []
            pq_war = {0: None, 1: None}
            for c in range(DC):
                sl = c % 2
                deps = [rtoks[6 + sl] if c < 2 else None, pq_war[sl]]
                if sl == 0:
                    def em(e, c=c, sl=sl):
                        return e.dma_start(out=sc1[:, sl * 1024:(sl + 1) * 1024],
                                           in_=wt_h[c * 128:(c + 1) * 128, :]).then_inc(LD, 16)
                    DMA(em, *deps)
                    td2 = prog.snap("ld")
                else:
                    def em(e, c=c, sl=sl):
                        return e.dma_start(out=sc1[:, sl * 1024:(sl + 1) * 1024],
                                           in_=wt_h[c * 128:(c + 1) * 128, :]).then_inc(LS, 16)
                    DMAS(em, *deps)
                    td2 = prog.snap("ls")
                t1 = S(lambda e, sl=sl: e.activation(out=eT[:, 0:1024],
                                                     in_=sc1[:, sl * 1024:(sl + 1) * 1024],
                                                     func=ACT.Sign), td2, last, rt[1])
                t2 = S(lambda e, sl=sl: e.activation(out=qTf32[:, 1024:2048],
                                                     in_=sc1[:, sl * 1024:(sl + 1) * 1024],
                                                     func=ACT.Abs), td2, rt[1])
                pq_war[sl] = [t1, t2]
                t3 = V(lambda e: e.tensor_scalar(out=eT[:, 1024:2048], in0=qTf32[:, 1024:2048],
                                                 scalar1=colp[:, 130:131], scalar2=None,
                                                 op0=ALU.is_gt), t2, tok_d)
                last = V(lambda e, c=c: e.tensor_tensor(out=wchunk(1, c, 0, 1024),
                                                        in0=eT[:, 0:1024],
                                                        in1=eT[:, 1024:2048],
                                                        op=ALU.mult), t1, t3)
                qtoks.append(last)
            sc1_war[0] = last
            sc1_war[1] = last
            return last, qtoks

        def ternary_sublayer(first):
            rt = ln_stats(xap, alpha_scale=True)
            tok_tw, qtoks = ternary_prep(rt) if first else (None, None)
            tap("tern_w", wslot(1)[:, 0:8192], BF16, tok_tw)
            tap("tern_rows", rows2[:, :], F32, rt[0], rt[1])
            tap("tern_delta", colp[:, 130:131], F32, prog.snap("v"))
            bct = BC(rt, None)
            last_ev = None
            for ti in range(NT):
                o = ti * TILE
                bct.load(ti)
                tok_bc = bct.tok[ti]
                tok_xn1 = None
                for c in range(DC):
                    def e1(e, c=c, o=o, ti=ti):
                        return e.tensor_tensor(out=sc2[:, 0:512], in0=xap(c)[:, o:o + TILE],
                                               in1=bct.nmu(ti), op=ALU.add)
                    deps = [tok_bc]
                    if ti > 0 and c == 0:
                        deps.append(last_ev)
                    d1 = V(e1, *deps)
                    def e2(e, c=c, ti=ti):
                        return e.tensor_tensor(out=eT[:, c * 512:(c + 1) * 512],
                                               in0=sc2[:, 0:512],
                                               in1=bct.rsd(ti), op=ALU.mult)
                    tok_xn1 = V(e2, d1)
                bct.done(ti, tok_xn1)
                if ti == 0:
                    tap("tern_xn1", eT[:, :], BF16, tok_xn1)
                ev_rd = {}
                for m in range(DC):
                    half = m % 2
                    for c in range(DC):
                        def em(e, m=m, c=c, half=half):
                            return e.matmul(out=pW0[:, half * 512:(half + 1) * 512],
                                            lhsT=wchunk(1, c, m * 128, (m + 1) * 128),
                                            rhs=eT[:, c * 512:(c + 1) * 512],
                                            start=(c == 0), stop=(c == DC - 1))
                        deps = [qtoks[c] if qtoks is not None else None]
                        if c == 0:
                            deps.append(tok_xn1)
                            if m >= 2:
                                deps.append(ev_rd[m - 2])
                        tb = T(em, *deps)
                    def sbias(e, half=half, m=m):
                        return e.activation(out=pW1[:, half * 512:(half + 1) * 512],
                                            in_=pW0[:, half * 512:(half + 1) * 512],
                                            func=ACT.Identity,
                                            bias=colp[:, CB["biast"] + m:CB["biast"] + m + 1])
                    ts_ = S(sbias, tb, ev_rd.get(m - 2))
                    def ev(e, m=m, o=o, half=half):
                        return e.tensor_tensor(out=xap(m)[:, o:o + TILE],
                                               in0=pW1[:, half * 512:(half + 1) * 512],
                                               in1=xap(m)[:, o:o + TILE], op=ALU.add)
                    last_ev = V(ev, ts_)
                    ev_rd[m] = last_ev
            prog.barrier()

        if N_SUBLAYERS >= 2:
            ternary_sublayer(first=True)

        if N_SUBLAYERS >= 3:
            rt3 = ln_stats(xap)
            war_t = prog.snap("t")
            DMA(lambda e: e.dma_start(out=sc1[:, 0:768], in_=wxp_h).then_inc(LD, 16), rt3[1],
                sc1_war[0])
            tok_wl = prog.snap("ld")
            DMA(lambda e: e.dma_start(out=sc1[0:64, 1024:2048], in_=wdt_h).then_inc(LD, 16),
                sc1_war[1])
            tok_wl2 = prog.snap("ld")
            tok_wxp = V(lambda e: e.tensor_copy(out=wxpv, in_=sc1[:, 0:768]), tok_wl, war_t)
            tok_wdt = V(lambda e: e.tensor_copy(out=wdtv, in_=sc1[0:64, 1024:2048]), tok_wl2)
            sc1_war[0] = tok_wxp
            sc1_war[1] = tok_wdt
            tsel0 = V(lambda e: e.memset(wW[64:96, 2048:6144], 0.0))
            DMA(lambda e: e.dma_start(out=wW[64:65, 2048:2176],
                                      in_=onesr[0:1, 0:128]).then_inc(LD, 16), tsel0)
            for k in (1, 2, 4, 8, 16):
                DMA(lambda e, k=k: e.dma_start(
                    out=wW[64 + k:64 + 2 * k, 2048 + 128 * k:2048 + 256 * k],
                    in_=wW[64:64 + k, 2048:2048 + 128 * k]).then_inc(LD, 16))
            tok_sel = prog.snap("ld")

            def selw(w):
                return wW[64:96, 2048 + 128 * w:2176 + 128 * w]

            bc5 = BC(rt3, None)
            xp_last = None
            for ti in range(NT):
                o = ti * TILE
                bc5.load(ti)
                tok_bc = bc5.tok[ti]
                half = ti % 2
                ln_toks = {}
                for c in range(DC):
                    def e1(e, c=c, o=o, ti=ti):
                        return e.tensor_tensor(out=sc2[:, 0:512], in0=xap(c)[:, o:o + TILE],
                                               in1=bc5.nmu(ti), op=ALU.add)
                    deps = [tok_bc]
                    if ti >= 2 and c == 0:
                        deps.append(xp_last)
                    d1 = V(e1, *deps)
                    def e2(e, c=c, ti=ti):
                        return e.tensor_tensor(out=lnD[:, c * 512:(c + 1) * 512],
                                               in0=sc2[:, 0:512], in1=bc5.rsd(ti), op=ALU.mult)
                    ln_toks[c] = V(e2, d1)
                bc5.done(ti, ln_toks[DC - 1])
                txp = None
                for c in range(DC):
                    def em(e, c=c, half=half):
                        return e.matmul(out=pW0[0:96, half * 512:(half + 1) * 512],
                                        lhsT=wxpv[:, c * 96:(c + 1) * 96],
                                        rhs=lnD[:, c * 512:(c + 1) * 512],
                                        start=(c == 0), stop=(c == DC - 1))
                    deps = [ln_toks[c], tok_wxp]
                    if ti >= 2 and c == 0:
                        deps.append(xp_last)
                    txp = T(em, *deps)
                tv1 = V(lambda e, o=o, half=half: e.tensor_copy(
                    out=dtlow_bf[:, o:o + TILE],
                    in_=pW0[0:64, half * 512:(half + 1) * 512]), txp, rt3[0], rt3[1])
                xp_last = V(lambda e, o=o, half=half: e.tensor_copy(
                    out=zbbf[64:96, o:o + TILE],
                    in_=pW0[64:96, half * 512:(half + 1) * 512]), tv1)
            tok_xp = xp_last
            prog.barrier()

            dtB_ = [xB[:, 0:L], xB[:, 5 * L:6 * L]]
            yB_ = [xB[:, L:L + 1024].bitcast(BF16),
                   xB[:, L + 1024:2 * L].bitcast(BF16)]
            AbB_ = [xB[:, 2 * L:3 * L], xB[:, 3 * L:4 * L]]
            hB_ = [xB[:, 4 * L:4 * L + 1024].bitcast(BF16),
                   xB[:, 4 * L + 1024:5 * L].bitcast(BF16)]
            tokB_ = [xB[:, 6 * L:6 * L + 1024].bitcast(BF16),
                     xB[:, 6 * L + 1024:7 * L].bitcast(BF16)]
            prB_ = [xB[:, 7 * L:7 * L + 1024].bitcast(BF16),
                    xB[:, 7 * L + 1024:8 * L].bitcast(BF16)]
            ln2c = lnD[:, 0:4096].bitcast(F32)
            uB = lnD[:, 4096:6144]
            zcol = sc2[:, 1023:1024]
            tz = V(lambda e: e.memset(zcol, 0.0), rt3[0])
            tap("Acol_t", Acol[:, :], F32, tz)
            bc6 = BC(rt3, None)
            slab_last = [None, None]
            pw0_rd = {0: None, 1: None}
            pw1_rd = {0: None, 1: None}
            scan_done = [None, None]
            prod_done = [None, None]
            yadd_done = [None, None]
            dt_rd = {}
            bc6_cnt = [0]
            dt_war = [None, None]
            y_war = [None, None]
            sp_war = [None, None]
            st = {}

            def prep_chunk(c):
                buf = c % 2
                dtB = dtB_[buf]
                yB = yB_[buf]
                ln_tok = None
                tok_dt = None
                for ti in range(NT):
                    o = ti * TILE
                    bidx = bc6_cnt[0]; bc6_cnt[0] += 1
                    bc6.load(bidx)
                    tok_bc = bc6.tok[bidx]
                    def e1(e, c=c, o=o, bidx=bidx):
                        return e.tensor_tensor(out=sc2[:, 0:512], in0=xap(c)[:, o:o + TILE],
                                               in1=bc6.nmu(bidx), op=ALU.add)
                    d1 = V(e1, tok_bc)
                    yield
                    def e2(e, o=o, bidx=bidx):
                        return e.tensor_tensor(out=ln2c[:, o:o + TILE],
                                               in0=sc2[:, 0:512], in1=bc6.rsd(bidx), op=ALU.mult)
                    ln_tok = V(e2, d1)
                    bc6.done(bidx, ln_tok)
                    idx = c * NT + ti
                    b0 = (ti % 2) * 1024
                    def em(e, c=c, o=o):
                        return e.matmul(out=pKV[:, 0:512],
                                        lhsT=wdtv[:, c * 128:(c + 1) * 128],
                                        rhs=dtlow_bf[:, o:o + TILE],
                                        start=True, stop=True)
                    deps = [tok_xp, tok_wdt]
                    if idx >= 1:
                        deps.append(dt_rd[idx - 1])
                    tm = T(em, *deps)
                    def s1(e, c=c, b0=b0):
                        return e.activation(out=sc1[:, b0:b0 + 512],
                                            in_=pKV[:, 0:512],
                                            func=ACT.Identity,
                                            bias=colp[:, CB["bdt"] + c:CB["bdt"] + c + 1])
                    a1 = S(s1, tm, sp_war[ti % 2])
                    dt_rd[idx] = a1
                    a2 = S(lambda e, b0=b0: e.activation(out=sc1[:, b0 + 512:b0 + 1024],
                                                         in_=sc1[:, b0:b0 + 512],
                                                         func=ACT.Abs), a1)
                    a3 = S(lambda e, b0=b0: e.activation(out=sc1[:, b0 + 512:b0 + 1024],
                                                         in_=sc1[:, b0 + 512:b0 + 1024],
                                                         func=ACT.Exp, scale=-1.0), a2)
                    a4 = S(lambda e, b0=b0: e.activation(out=sc1[:, b0 + 512:b0 + 1024],
                                                         in_=sc1[:, b0 + 512:b0 + 1024],
                                                         func=ACT.Ln, bias=colp[:, 129:130]), a3)
                    a5 = S(lambda e, b0=b0: e.activation(out=sc1[:, b0:b0 + 512],
                                                         in_=sc1[:, b0:b0 + 512],
                                                         func=ACT.Relu), a4)
                    def v1(e, o=o, b0=b0, dtB=dtB):
                        return e.tensor_tensor(out=dtB[:, o:o + TILE], in0=sc1[:, b0:b0 + 512],
                                               in1=sc1[:, b0 + 512:b0 + 1024], op=ALU.add)
                    tok_dt = V(v1, a5, dt_war[buf])
                    sp_war[ti % 2] = tok_dt
                    yield
                def ey(e, c=c, yB=yB):
                    return e.tensor_scalar(out=yB, in0=ln2c,
                                           scalar1=colp[:, CB["dvec"] + c:CB["dvec"] + c + 1],
                                           scalar2=None, op0=ALU.mult)
                tok_y = V(ey, ln_tok, tok_dt, y_war[buf])
                st[c] = (tok_dt, ln_tok, tok_y)
                yield

            g = prep_chunk(0)
            for _ in g:
                pass
            for c in range(DC):
                cbuf = c % 2
                dtB = dtB_[cbuf]
                yB = yB_[cbuf]
                tok_dt, ln_tok, tok_y = st[c]
                g = prep_chunk(c + 1) if c + 1 < DC else None
                def eu(e, dtB=dtB):
                    return e.tensor_tensor(out=uB, in0=dtB, in1=ln2c, op=ALU.mult)
                tok_u = V(eu, tok_dt, ln_tok)
                prev = None
                ta = None
                tt_ = None
                for si in range(DS):
                    buf = si % 2
                    AbB = AbB_[buf]; tokB = tokB_[buf]; hB = hB_[buf]; prB = prB_[buf]
                    bmslab = qT[:, buf * 2048:(buf + 1) * 2048]
                    cmslab = eT[:, buf * 2048:(buf + 1) * 2048]
                    tok_slab = None
                    tok_cslab = None
                    for q in range(4):
                        h2 = q % 2
                        def emb(e, si=si, q=q, h2=h2):
                            return e.matmul(out=pW0[:, h2 * 512:(h2 + 1) * 512],
                                            lhsT=selw(si),
                                            rhs=zbbf[64:96, q * 512:(q + 1) * 512],
                                            start=True, stop=True)
                        tb = T(emb, tok_xp, tok_sel, pw0_rd[h2])
                        def esb(e, q=q, h2=h2, bmslab=bmslab):
                            return e.activation(out=bmslab[:, q * 512:(q + 1) * 512],
                                                in_=pW0[:, h2 * 512:(h2 + 1) * 512],
                                                func=ACT.Identity)
                        tok_slab = S(esb, tb, slab_last[buf])
                        pw0_rd[h2] = tok_slab
                    for q in range(4):
                        h2 = q % 2
                        def emc(e, si=si, q=q, h2=h2):
                            return e.matmul(out=pW1[:, h2 * 512:(h2 + 1) * 512],
                                            lhsT=selw(16 + si),
                                            rhs=zbbf[64:96, q * 512:(q + 1) * 512],
                                            start=True, stop=True)
                        tc = T(emc, tok_xp, tok_sel, pw1_rd[h2])
                        def esc(e, q=q, h2=h2, cmslab=cmslab):
                            return e.activation(out=cmslab[:, q * 512:(q + 1) * 512],
                                                in_=pW1[:, h2 * 512:(h2 + 1) * 512],
                                                func=ACT.Identity)
                        tok_cslab = S(esc, tc, slab_last[buf])
                        pw1_rd[h2] = tok_cslab
                    def ea(e, c=c, si=si, AbB=AbB, dtB=dtB):
                        return e.activation(out=AbB, in_=dtB, func=ACT.Exp,
                                            scale=Acol[:, c * 16 + si:c * 16 + si + 1])
                    ta = S(ea, tok_dt, scan_done[buf])
                    def et(e, bmslab=bmslab, tokB=tokB):
                        return e.tensor_tensor(out=tokB, in0=uB, in1=bmslab, op=ALU.mult)
                    tt_ = V(et, tok_u, tok_slab, scan_done[buf])
                    def eh(e, AbB=AbB, tokB=tokB, hB=hB):
                        return e.tensor_tensor_scan(out=hB, data0=AbB, data1=tokB,
                                                    initial=zcol, op0=ALU.mult, op1=ALU.add)
                    th = V(eh, ta, tt_, tz, prod_done[buf])
                    scan_done[buf] = th
                    def ep(e, cmslab=cmslab, hB=hB, prB=prB):
                        return e.tensor_tensor(out=prB, in0=hB, in1=cmslab, op=ALU.mult)
                    tp = V(ep, th, yadd_done[buf], tok_cslab)
                    prod_done[buf] = tp
                    slab_last[buf] = tp
                    def ey2(e, prB=prB, yB=yB):
                        return e.tensor_tensor(out=yB, in0=yB, in1=prB, op=ALU.add)
                    prev = V(ey2, tp, tok_y, prev)
                    yadd_done[buf] = prev
                    if g is not None:
                        next(g, None)
                if g is not None:
                    for _ in g:
                        pass
                dt_war[cbuf] = ta
                txa = V(lambda e, c=c, yB=yB: e.tensor_tensor(out=xap(c), in0=xap(c), in1=yB,
                                                              op=ALU.add), prev)
                y_war[cbuf] = txa
            prog.barrier()

        if N_SUBLAYERS >= 4:
            ternary_sublayer(first=False)

        for c in range(DC):
            DMAO(lambda e, c=c, seq=seq: e.dma_start(
                out=out_h[seq, c], in_=xA[:, c * L:(c + 1) * L]).then_inc(SO, 16))
        prog.barrier()

    n_so = prog.cnt["so"]
    prog.plan["y"].append(([("so", n_so)], lambda e: e.nop(), None))

    with nc.Block() as block:
        def make(engname):
            def body(eng):
                wm = {}
                for deps, emit, inctok in prog.plan[engname]:
                    for (sn, val) in deps:
                        if val > wm.get(sn, 0):
                            eng.wait_ge(sems[sn], val)
                            wm[sn] = val
                    inst = emit(eng)
                    if inctok is not None:
                        inst.then_inc(sems[inctok[0]], 1)
            return body
        block.vector(make("v"))
        block.scalar(make("s"))
        block.tensor(make("t"))
        if prog.plan["g"]:
            block.gpsimd(make("g"))
        block.sync(make("y"))

    for h in reversed(ctx):
        h.__exit__(None, None, None)
    return nc


_NC_CACHE = {}


def kernel(**inputs):
    if "nc" not in _NC_CACHE:
        _NC_CACHE["nc"] = build_nc()
    nc = _NC_CACHE["nc"]

    x = np.asarray(inputs["L0_attn_a_x"])
    asc = np.ascontiguousarray
    f32 = np.float32
    shared = {
        "wq": asc(np.asarray(inputs["L0_attn_a_Wq"], dtype=f32)),
        "wk": asc(np.asarray(inputs["L0_attn_a_Wk"], dtype=f32)),
        "wv": asc(np.asarray(inputs["L0_attn_a_Wv"], dtype=f32)),
        "wo": asc(np.asarray(inputs["L0_attn_a_Wo"], dtype=f32)),
        "w1": asc(np.asarray(inputs["L0_attn_a_W1"], dtype=f32)),
        "w2": asc(np.asarray(inputs["L0_attn_a_W2"], dtype=f32)),
        "wt": asc(np.asarray(inputs["L0_ffn_weight"], dtype=f32).T),
        "wxp": asc(np.asarray(inputs["L0_attn_b_W_xproj"], dtype=f32).T
                   .reshape(8, 128, 96).transpose(1, 0, 2).reshape(128, 768)),
        "wdt": asc(np.asarray(inputs["L0_attn_b_W_dt"], dtype=f32).T),
        "alog": asc(np.asarray(inputs["L0_attn_b_A_log"], dtype=f32)
                    .reshape(8, 128, 16).transpose(1, 0, 2).reshape(128, 128)),
        "bq": asc(np.asarray(inputs["L0_attn_a_bq"], dtype=f32).reshape(8, 128).T),
        "bk": asc(np.asarray(inputs["L0_attn_a_bk"], dtype=f32)),
        "bv": asc(np.asarray(inputs["L0_attn_a_bv"], dtype=f32)),
        "bo": asc(np.asarray(inputs["L0_attn_a_bo"], dtype=f32).reshape(8, 128).T),
        "b1": asc(np.asarray(inputs["L0_attn_a_b1"], dtype=f32).reshape(32, 128).T),
        "b2": asc(np.asarray(inputs["L0_attn_a_b2"], dtype=f32).reshape(8, 128).T),
        "g1": asc(np.asarray(inputs["L0_attn_a_g1"], dtype=f32).reshape(8, 128).T),
        "be1": asc(np.asarray(inputs["L0_attn_a_beta1"], dtype=f32).reshape(8, 128).T),
        "g2": asc(np.asarray(inputs["L0_attn_a_g2"], dtype=f32).reshape(8, 128).T),
        "be2": asc(np.asarray(inputs["L0_attn_a_beta2"], dtype=f32).reshape(8, 128).T),
        "bdt": asc(np.asarray(inputs["L0_attn_b_b_dt"], dtype=f32).reshape(8, 128).T),
        "dvec": asc(np.asarray(inputs["L0_attn_b_D"], dtype=f32).reshape(8, 128).T),
        "biast": asc(np.asarray(inputs["L0_ffn_bias"], dtype=f32).reshape(8, 128).T),
        "alpha": asc(np.asarray(inputs["L0_ffn_alpha"], dtype=f32)),
    }
    in_maps = []
    for k in range(NCORES):
        xs = asc(x[2 * k:2 * k + 2].transpose(0, 2, 1)).reshape(NSEQ, DC, 128, L)
        m = dict(shared)
        m["x"] = xs
        in_maps.append(m)
    res = run_bass_kernel_spmd(nc, in_maps, core_ids=list(range(NCORES)))
    if DEBUG:
        global DBG_OUT
        DBG_OUT = {k: [np.asarray(res.results[i][k]) for i in range(NCORES)]
                   for k in res.results[0] if k.startswith("dbg_")}
    outs = []
    for k in range(NCORES):
        o = np.asarray(res.results[k]["o"]).reshape(NSEQ, D, L).transpose(0, 2, 1)
        outs.append(o)
    return np.concatenate(outs, axis=0).astype(np.float32)


# version 29 (final)
# speedup vs baseline: 1.1218x; 1.0120x over previous
import numpy as np
import concourse.bass as bass
import concourse.mybir as mybir
from concourse.bass_utils import run_bass_kernel_spmd

F32 = mybir.dt.float32
BF16 = mybir.dt.bfloat16
ALU = mybir.AluOpType
ACT = mybir.ActivationFunctionType
AX = mybir.AxisListType

NCORES = 8
NSEQ = 2
L = 2048
D = 1024
DC = 8
NH = 16
TILE = 512
NT = L // TILE
TT = 128
NTT = L // TT
FT = 256
NFT = L // FT
JG = 4
DS = 16
EPS_LN = 1e-5
EPS_AT = 1e-6

N_SUBLAYERS = 4
DEBUG = False
DBG_OUT = {}


class Prog:
    ENGS = ("v", "s", "t", "g", "y")

    def __init__(self):
        self.plan = {e: [] for e in self.ENGS}
        self.cnt = {"v": 0, "s": 0, "t": 0, "g": 0, "ld": 0, "ls": 0, "lg": 0, "so": 0}
        self.pending = {e: [] for e in self.ENGS}

    def step(self, eng, emit, deps=(), ndma=0, dmasem="ld"):
        flat = []
        for d in deps:
            if d is None:
                continue
            if isinstance(d, list):
                flat.extend(d)
            else:
                flat.append(d)
        deps = flat + self.pending[eng]
        self.pending[eng] = []
        if ndma:
            self.cnt[dmasem] += 16 * ndma
            tok = (dmasem, self.cnt[dmasem])
            self.plan[eng].append((deps, emit, None))
        else:
            self.cnt[eng] += 1
            tok = (eng, self.cnt[eng])
            self.plan[eng].append((deps, emit, tok))
        return tok

    def snap(self, sem):
        return (sem, self.cnt[sem])

    def barrier(self):
        toks = [(k, v) for k, v in self.cnt.items() if v > 0]
        for e in self.ENGS:
            self.pending[e] = list(toks)


def build_nc():
    nc = bass.Bass("TRN2", target_bir_lowering=False, dynamic_dma_scratch_size=256)

    def di(name, shape):
        return nc.dram_tensor(name, shape, F32, kind="ExternalInput").ap()
    x_h = di("x", [NSEQ, DC, 128, L])
    wq_h = di("wq", [D, D]); wk_h = di("wk", [D, D])
    wv_h = di("wv", [D, D]); wo_h = di("wo", [D, D])
    w1_h = di("w1", [D, 4 * D]); w2_h = di("w2", [4 * D, D])
    wt_h = di("wt", [D, D])
    wxp_h = di("wxp", [128, 768])
    wdt_h = di("wdt", [64, D])
    alog_h = di("alog", [128, 128])
    cnames = ["bq", "bo", "b2", "g1", "be1", "g2", "be2", "bdt", "dvec", "biast"]
    vecs = {n: di(n, [128, 8]) for n in cnames}
    vecs["bk"] = di("bk", [D]); vecs["bv"] = di("bv", [D])
    b1_h = di("b1", [128, 32])
    alpha_h = di("alpha", [1])
    out_h = nc.dram_tensor("o", [NSEQ, DC, 128, L], F32, kind="ExternalOutput").ap()
    dbg_h = {}

    ctx = []
    def sb(name, shape, dt=F32):
        h = nc.sbuf_tensor(name, shape, dt); t = h.__enter__(); ctx.append(h); return t
    def psum(name, shape, dt=F32):
        h = nc.psum_tensor(name, shape, dt); t = h.__enter__(); ctx.append(h); return t
    def semh(name):
        h = nc.semaphore(name); s = h.__enter__(); ctx.append(h); return s

    xA = sb("xA", [128, DC * L])
    xB = sb("xB", [128, DC * L])
    lnD = sb("lnD", [128, 6144], BF16)
    wW = sb("wW", [128, 2 * 8192], BF16)
    sc1 = sb("sc1", [128, 2048])
    sc2 = sb("sc2", [128, 1024])
    qT = sb("qT", [128, 4096], BF16)
    eT = sb("eT", [128, 4096], BF16)
    zb = sb("zb", [128, 1024])
    nmu_bc = sb("nmu_bc", [128, 1024])
    rsd_bc = sb("rsd_bc", [128, 1024])
    rows2 = sb("rows2", [128, 2048])
    colp = sb("colp", [128, 152])
    zxsel = sb("zxsel", [128, 128])
    Acol = sb("Acol", [128, 128])
    kvE = sb("kvE", [128, 512], BF16)
    ksE = sb("ksE", [128, 16], BF16)
    onesc = sb("onesc", [128, 1], BF16)
    onesr = sb("onesr", [1, 128], BF16)
    zrow = sb("zrow", [1, 128], BF16)
    CB = {"bq": 8, "negbq": 16, "bo": 24, "b2": 32, "g1": 40, "be1": 48,
          "g2": 56, "be2": 64, "dvec": 72, "bdt": 80, "biast": 88}

    pW0 = psum("pW0", [128, 1024])
    pW1 = psum("pW1", [128, 1024])
    pKV = psum("pKV", [128, 512])
    pKs = psum("pKs", [128, 8])
    pDen = psum("pDen", [2, 512])
    pSt = psum("pSt", [128, 512])

    sems = {n: semh(n) for n in ("v", "s", "t", "g", "ld", "ls", "lg", "so")}
    prog = Prog()

    def V(emit, *deps): return prog.step("v", emit, deps)
    def S(emit, *deps): return prog.step("s", emit, deps)
    def T(emit, *deps): return prog.step("t", emit, deps)
    def G(emit, *deps): return prog.step("g", emit, deps)
    def DMA(emit, *deps): return prog.step("y", emit, deps, ndma=1)
    def DMAS(emit, *deps): return prog.step("s", emit, deps, ndma=1, dmasem="ls")
    def DMAG(emit, *deps): return prog.step("g", emit, deps, ndma=1, dmasem="lg")
    def DMAO(emit, *deps): return prog.step("y", emit, deps, ndma=1, dmasem="so")
    LD = sems["ld"]; LS = sems["ls"]; SO = sems["so"]; LG = sems["lg"]

    def tap(name, ap, dt, *deps):
        if not DEBUG or name in dbg_h:
            return
        shape = list(ap.shape)
        dbg_h[name] = nc.dram_tensor("dbg_" + name, shape, dt, kind="ExternalOutput").ap()
        DMAO(lambda e, ap=ap, name=name: e.dma_start(out=dbg_h[name], in_=ap).then_inc(SO, 16),
             *deps)

    sc2bf = sc2[:, :].bitcast(BF16)
    qTf32 = qT[:, :].bitcast(F32)
    eTf32 = eT[:, :].bitcast(F32)
    zbbf = zb[:, :].bitcast(BF16)
    dtlow_bf = zbbf[0:64, 0:2048]

    def bcast(row_ap, p=128, n=L):
        return row_ap.unsqueeze(1).broadcast_to([row_ap.shape[0], p, n])

    def wslot(i):
        return wW[:, i * 8192:(i + 1) * 8192]

    def wchunk(slot, c, a, b):
        return wslot(slot)[:, c * 1024 + a: c * 1024 + b]

    wxpv = wslot(0)[:, 0:768]
    wdtv = wslot(0)[0:64, 1024:2048]

    def ld_col(nm, cc):
        DMA(lambda e, nm=nm, cc=cc: e.dma_start(
            out=colp[:, cc:cc + 8], in_=vecs[nm]).then_inc(LD, 16))
    for nm, cc in [("bq", 8), ("bo", 24), ("b2", 32), ("g1", 40), ("be1", 48),
                   ("g2", 56), ("be2", 64), ("dvec", 72), ("bdt", 80), ("biast", 88)]:
        ld_col(nm, cc)
    DMA(lambda e: e.dma_start(out=colp[:, 96:128], in_=b1_h).then_inc(LD, 16))
    DMA(lambda e: e.dma_start(
        out=colp[:, 128:129],
        in_=bcast(alpha_h.rearrange("(a b) -> a b", a=1), 128, 1)).then_inc(LD, 16))
    DMA(lambda e: e.dma_start(out=Acol[:, :], in_=alog_h).then_inc(LD, 16))
    tok_ld0 = prog.snap("ld")

    V(lambda e: e.memset(onesc[:], 1.0))
    V(lambda e: e.memset(colp[:, 131:132], 1.0))
    V(lambda e: e.memset(onesr[:], 1.0))
    V(lambda e: e.memset(zrow[:], 0.0))
    V(lambda e: e.memset(colp[:, 1:2], EPS_LN))
    V(lambda e: e.memset(colp[:, 3:4], EPS_AT))
    V(lambda e: e.memset(colp[:, 129:130], 1.0))
    V(lambda e: e.tensor_scalar(out=colp[:, 16:24], in0=colp[:, 8:16],
                                scalar1=-1.0, scalar2=None, op0=ALU.mult), tok_ld0)
    t_ea = S(lambda e: e.activation(out=sc1[:, 0:128], in_=Acol[:, :], func=ACT.Exp), tok_ld0)
    V(lambda e: e.tensor_scalar(out=Acol[:, :], in0=sc1[:, 0:128],
                                scalar1=-1.0, scalar2=None, op0=ALU.mult), t_ea)
    prog.barrier()

    sc1_war = [None, None]
    wcast = {0: [None] * 8, 1: [None] * 8}

    def load_weight_1024(slot, hbm, rows0=0, cols0=0, dep0=None):
        war = prog.snap("t")
        last = None
        half = [sc1_war[0], sc1_war[1]]
        for c in range(8):
            sl = c % 2
            deps = [half[sl], dep0 if c < 2 else None]
            def em(e, c=c, sl=sl, hbm=hbm, rows0=rows0, cols0=cols0):
                return e.dma_start(out=sc1[:, sl * 1024:(sl + 1) * 1024],
                                   in_=hbm[rows0 + c * 128: rows0 + (c + 1) * 128,
                                           cols0:cols0 + 1024]).then_inc(LD, 16)
            DMA(em, *deps)
            tdma = prog.snap("ld")
            def ec(e, c=c, sl=sl, slot=slot):
                return e.activation(out=wchunk(slot, c, 0, 1024),
                                    in_=sc1[:, sl * 1024:(sl + 1) * 1024], func=ACT.Identity)
            tcast = S(ec, tdma, war if c == 0 else None)
            half[sl] = tcast
            last = tcast
            wcast[slot][c] = tcast
        sc1_war[0] = half[0]
        sc1_war[1] = half[1]
        return last

    pst_war = [None]

    def ln_stats(src, alpha_scale=False, wdeps=()):
        sxr = qTf32[0:1, 0:2048]
        sqr = eTf32[0:1, 0:2048]
        m2r = sc1[0:1, 0:2048]
        last_mm = None
        last_ev = None
        stg_war = [None] * DC
        for piece in range(8):
            o = piece * 256
            sq_toks = {}
            for c in range(DC):
                def esq(e, o=o, c=c):
                    return e.activation(out=sc2bf[:, c * 256:(c + 1) * 256],
                                        in_=src(c)[:, o:o + 256], func=ACT.Square)
                deps = list(wdeps) if (piece == 0 and c == 0) else []
                deps.append(stg_war[c])
                sq_toks[c] = S(esq, *deps)
            for c in range(DC):
                def emx(e, c=c, o=o):
                    return e.matmul(out=pSt[0:1, 0:256], lhsT=colp[:, 131:132],
                                    rhs=src(c)[:, o:o + 256],
                                    start=(c == 0), stop=(c == DC - 1))
                depsx = list(wdeps) if (piece == 0 and c == 0) else []
                if c == 0:
                    depsx.append(last_ev)
                    if piece == 0:
                        depsx.append(pst_war[0])
                tmx = T(emx, *depsx)
            for c in range(DC):
                def emq(e, c=c):
                    return e.matmul(out=pSt[0:1, 256:512], lhsT=onesc[:, :],
                                    rhs=sc2bf[:, c * 256:(c + 1) * 256],
                                    start=(c == 0), stop=(c == DC - 1))
                last_mm = T(emq, sq_toks[c])
                stg_war[c] = last_mm
            def ev1(e, o=o):
                return e.tensor_copy(out=sxr[:, o:o + 256], in_=pSt[0:1, 0:256])
            def ev2(e, o=o):
                return e.tensor_copy(out=sqr[:, o:o + 256], in_=pSt[0:1, 256:512])
            t1 = V(ev1, last_mm)
            last_ev = V(ev2, t1)
            pst_war[0] = last_ev
        tnm = V(lambda e: e.tensor_scalar(out=sxr, in0=sxr,
                                          scalar1=-1.0 / D, scalar2=None, op0=ALU.mult), last_ev)
        tm2 = S(lambda e: e.activation(out=m2r, in_=sxr, func=ACT.Square,
                                       scale=32.0), tnm)
        tvr = V(lambda e: e.tensor_tensor(out=sqr, in0=sqr, in1=m2r, op=ALU.subtract),
                last_ev, tm2)
        tln = S(lambda e: e.activation(out=sqr, in_=sqr, func=ACT.Ln, scale=1.0 / D,
                                       bias=colp[0:1, 1:2]), tvr)
        trs = S(lambda e: e.activation(out=sqr, in_=sqr, func=ACT.Exp, scale=-0.5), tln)
        tfin = trs
        if alpha_scale:
            tfin = V(lambda e: e.tensor_scalar(out=sqr, in0=sqr,
                                               scalar1=colp[0:1, 128:129], scalar2=None,
                                               op0=ALU.mult), trs)
        DMA(lambda e: e.dma_start(out=rows2[32:33, :], in_=sxr).then_inc(LD, 16), tnm, tfin)
        DMA(lambda e: e.dma_start(out=rows2[64:65, :], in_=sqr).then_inc(LD, 16), tfin)
        tok = prog.snap("ld")
        return (tok, tok)

    class BC:
        def __init__(self, rows_tok, global_war):
            self.rows_tok = rows_tok
            self.war = {0: global_war, 1: global_war}
            self.tok = {}

        def load(self, ti):
            half = ti % 2
            o = (ti * 512) % L
            def m1(e, o=o):
                return e.matmul(out=pSt[:, :], lhsT=zxsel[32:33, 0:128],
                                rhs=rows2[32:33, o:o + 512], start=True, stop=True)
            t1 = T(m1, self.rows_tok[0], self.rows_tok[1], pst_war[0], tok_ones)
            def s1(e, half=half):
                return e.activation(out=nmu_bc[:, half * 512:(half + 1) * 512],
                                    in_=pSt[:, :], func=ACT.Identity)
            ts1 = S(s1, t1, self.war[half])
            def m2(e, o=o):
                return e.matmul(out=pSt[:, :], lhsT=zxsel[64:65, 0:128],
                                rhs=rows2[64:65, o:o + 512], start=True, stop=True)
            t2 = T(m2, ts1)
            def s2(e, half=half):
                return e.activation(out=rsd_bc[:, half * 512:(half + 1) * 512],
                                    in_=pSt[:, :], func=ACT.Identity)
            ts2 = S(s2, t2, self.war[half])
            pst_war[0] = ts2
            self.tok[ti] = [ts1, ts2]
            return self.tok[ti]

        def nmu(self, ti, a=0, b=512):
            half = ti % 2
            return nmu_bc[:, half * 512 + a: half * 512 + b]

        def rsd(self, ti, a=0, b=512):
            half = ti % 2
            return rsd_bc[:, half * 512 + a: half * 512 + b]

        def done(self, ti, tok):
            self.war[ti % 2] = tok

    def pclear(region_ap, ncols, *deps):
        t = None
        for k in range(0, ncols, 128):
            w = min(128, ncols - k)
            def em(e, k=k, w=w, region_ap=region_ap):
                return e.matmul(out=region_ap[:, k:k + w], lhsT=zrow[0:1, 0:128],
                                rhs=onesr[0:1, 0:w], start=True, stop=False)
            t = T(em, *deps)
        return t

    def load_kv_bias_rows(dep0):
        DMA(lambda e: e.dma_start(out=sc2[0:1, 0:1024],
                                  in_=vecs["bk"].rearrange("(a b) -> a b", a=1)).then_inc(LD, 16),
            dep0)
        t1 = V(lambda e: e.tensor_copy(out=zbbf[0:1, 0:1024], in_=sc2[0:1, 0:1024]),
               prog.snap("ld"))
        DMA(lambda e: e.dma_start(out=sc2[0:1, 0:1024],
                                  in_=vecs["bv"].rearrange("(a b) -> a b", a=1)).then_inc(LD, 16),
            t1)
        t2 = V(lambda e: e.tensor_copy(out=zbbf[0:1, 1024:2048], in_=sc2[0:1, 0:1024]),
               prog.snap("ld"))
        return t2

    xap = lambda c: xA[:, c * L:(c + 1) * L]
    xbap = lambda c: xB[:, c * L:(c + 1) * L]

    def xa2bf(c, a, b):
        if c < 2:
            return qT[:, c * 2048 + a: c * 2048 + b]
        if c < 4:
            return eT[:, (c - 2) * 2048 + a: (c - 2) * 2048 + b]
        if c == 4:
            return zbbf[:, a:b]
        if c < 7:
            return lnD[:, 2048 + (c - 5) * 2048 + a: 2048 + (c - 5) * 2048 + b]
        return sc2bf[:, a:b]

    tzx1 = V(lambda e: e.memset(zxsel[0:1, 0:64], 1.0))
    tzx2 = V(lambda e: e.memset(zxsel[0:1, 64:128], 0.0), tzx1)
    V(lambda e: e.memset(zxsel[32:33, 0:128], 1.0))
    V(lambda e: e.memset(zxsel[64:65, 0:128], 1.0))
    tok_ones = prog.snap("v")
    DMA(lambda e: e.dma_start(out=zxsel[1:2, 64:128],
                              in_=zxsel[0:1, 0:64]).then_inc(LD, 16), tzx2)
    DMA(lambda e: e.dma_start(out=zxsel[1:2, 0:64],
                              in_=zxsel[0:1, 64:128]).then_inc(LD, 16))
    tok_zxsel = prog.snap("ld")

    for seq in range(NSEQ):
        for c in range(DC):
            DMA(lambda e, c=c, seq=seq: e.dma_start(
                out=xA[:, c * L:(c + 1) * L], in_=x_h[seq, c]).then_inc(LD, 16))
        tok_x = prog.snap("ld")

        rt0 = ln_stats(xap, wdeps=[tok_x])
        tok_bkv = load_kv_bias_rows(rt0[1])
        tap("rows2", rows2[:, :], F32, rt0[0], rt0[1])
        tok_wk = load_weight_1024(0, wk_h, dep0=rt0[1])
        tok_wv = load_weight_1024(1, wv_h)
        tok_ckv = pclear(pKV[:, :], 512)
        tok_cks = pclear(pKs[:, :], 8)
        bc0 = BC(rt0, None)

        kv_mm_tok = None
        last_kevac = None
        last_vevac = None
        xn_war = {}
        for tt in range(NTT):
            o = tt * TT
            ti = tt // 4
            if tt % 4 == 0:
                bc0.load(ti)
            tok_bc = bc0.tok[ti]
            xn_tt = eT[:, 2048 + (tt % 2) * 1024:3072 + (tt % 2) * 1024]
            t_xn = None
            for c in range(DC):
                oin = (tt % 4) * 128
                def e1(e, c=c, o=o, ti=ti, oin=oin):
                    return e.tensor_tensor(out=sc2[:, 0:128], in0=xap(c)[:, o:o + TT],
                                           in1=bc0.nmu(ti, oin, oin + 128), op=ALU.add)
                deps = [tok_bc]
                if c == 0:
                    deps.append(xn_war.get(tt % 2))
                d1 = V(e1, *deps)
                def e2(e, c=c, ti=ti, oin=oin, xn_tt=xn_tt):
                    return e.tensor_tensor(out=xn_tt[:, c * 128:(c + 1) * 128], in0=sc2[:, 0:128],
                                           in1=bc0.rsd(ti, oin, oin + 128), op=ALU.mult)
                t_xn = V(e2, d1)
            bc0.done(ti, t_xn)
            tok_xn = t_xn
            if tt == 0:
                tap("xn_tt0", eT[:, 2048:3072], BF16, tok_xn)
            mm_last = None
            for proj, slot, pdst, wtok in (("k", 0, pW0, tok_wk), ("v", 1, pW1, tok_wv)):
                for h in range(2):
                    for c in range(DC):
                        def em(e, c=c, h=h, slot=slot, pdst=pdst, xn_tt=xn_tt):
                            return e.matmul(out=pdst[:, h * 512:(h + 1) * 512],
                                            lhsT=xn_tt[:, c * 128:(c + 1) * 128],
                                            rhs=wchunk(slot, c, h * 512, (h + 1) * 512),
                                            start=(c == 0), stop=False)
                        deps = [wcast[slot][c]]
                        if c == 0:
                            deps += [tok_xn,
                                     last_kevac if slot == 0 else last_vevac]
                        mm_last = T(em, *deps)
                    def eb(e, h=h, proj=proj, pdst=pdst):
                        off = 0 if proj == "k" else 1024
                        return e.matmul(out=pdst[:, h * 512:(h + 1) * 512],
                                        lhsT=onesr[0:1, 0:128],
                                        rhs=zbbf[0:1, off + h * 512: off + (h + 1) * 512],
                                        start=False, stop=True)
                    deps = [tok_bkv] if tt == 0 else []
                    mm_last = T(eb, *deps)
            tok_kvproj = mm_last
            k_tm = eT[:, 0:1024]
            v_tm = eT[:, 1024:2048]
            ts1 = S(lambda e: e.activation(out=qT[:, 0:1024], in_=pW0[:, :], func=ACT.Relu),
                    tok_kvproj)
            ts3 = None
            for h2 in range(2):
                def es2(e, h2=h2):
                    return e.activation(out=sc2[:, 512:1024],
                                        in_=pW0[:, h2 * 512:(h2 + 1) * 512],
                                        func=ACT.Relu, scale=-1.0)
                ts2 = S(es2, tok_kvproj if h2 == 0 else ts3)
                def es3(e, h2=h2):
                    return e.activation(out=qT[:, 1024 + h2 * 512:1536 + h2 * 512],
                                        in_=sc2[:, 512:1024], func=ACT.Exp, scale=-1.0)
                ts3 = S(es3, ts2)
            tk = V(lambda e: e.tensor_tensor(out=k_tm[:, :], in0=qT[:, 0:1024],
                                             in1=qT[:, 1024:2048], op=ALU.add),
                   ts1, ts3, kv_mm_tok)
            last_kevac = tk
            tv = S(lambda e: e.activation(out=v_tm[:, :], in_=pW1[:, :], func=ACT.Identity),
                   tok_kvproj, kv_mm_tok)
            last_vevac = tv
            if tt == 0:
                tap("kv_tt0", eT[:, 0:2048], BF16, tk, tv)
            for hh in range(NH):
                def ekv(e, hh=hh):
                    return e.matmul(out=pKV[64 * (hh % 2):64 * (hh % 2) + 64,
                                          (hh // 2) * 64:(hh // 2) * 64 + 64],
                                    lhsT=k_tm[:, hh * 64:(hh + 1) * 64],
                                    rhs=v_tm[:, hh * 64:(hh + 1) * 64],
                                    start=False, stop=(tt == NTT - 1))
                deps = [tk, tv] if hh == 0 else []
                if hh == 0 and tt == 0:
                    deps += [tok_ckv, tok_cks]
                T(ekv, *deps)
                def eks(e, hh=hh):
                    return e.matmul(out=pKs[64 * (hh % 2):64 * (hh % 2) + 64,
                                          (hh // 2):(hh // 2) + 1],
                                    lhsT=k_tm[:, hh * 64:(hh + 1) * 64],
                                    rhs=onesc[:, :],
                                    start=False, stop=(tt == NTT - 1))
                kv_mm_tok = T(eks)
            xn_war[tt % 2] = kv_mm_tok
        tkve = V(lambda e: e.tensor_copy(out=kvE[:, :], in_=pKV[:, :]), kv_mm_tok)
        tks0 = V(lambda e: e.memset(ksE[:, :], 0.0))
        tok_kv = tkve
        for hh in range(NH):
            def ek(e, hh=hh):
                return e.tensor_copy(out=ksE[64 * (hh % 2):64 * (hh % 2) + 64, hh:hh + 1],
                                     in_=pKs[64 * (hh % 2):64 * (hh % 2) + 64,
                                             (hh // 2):(hh // 2) + 1])
            tok_kv = V(ek, kv_mm_tok, tks0)
        tap("kvE", kvE[:, :], BF16, tok_kv)
        tap("ksE", ksE[:, :], BF16, tok_kv)
        prog.barrier()

        tok_wq = load_weight_1024(0, wq_h)
        tok_wo = load_weight_1024(1, wo_h)

        zmm_rd = None
        bc2 = BC(rt0, tok_xn)
        last_wo_ev = None
        for ti in range(NT):
            o = ti * TILE
            bc2.load(ti)
            tok_bc = bc2.tok[ti]
            xn_bf = eT
            t_xn = None
            for c in range(DC):
                def e1(e, c=c, o=o, ti=ti):
                    return e.tensor_tensor(out=sc2[:, 0:512], in0=xap(c)[:, o:o + TILE],
                                           in1=bc2.nmu(ti), op=ALU.add)
                deps = [tok_bc]
                if ti > 0 and c == 0:
                    deps.append(last_wo_ev)
                d1 = V(e1, *deps)
                def e2(e, c=c, ti=ti):
                    return e.tensor_tensor(out=xn_bf[:, c * 512:(c + 1) * 512], in0=sc2[:, 0:512],
                                           in1=bc2.rsd(ti), op=ALU.mult)
                t_xn = V(e2, d1)
            bc2.done(ti, t_xn)
            tok_xn = t_xn
            q_rd = {}
            tok_q = None
            for m in range(DC):
                half = m % 2
                for c in range(DC):
                    def em(e, m=m, c=c, half=half):
                        return e.matmul(out=pW0[:, half * 512:(half + 1) * 512],
                                        lhsT=wchunk(0, c, m * 128, (m + 1) * 128),
                                        rhs=xn_bf[:, c * 512:(c + 1) * 512],
                                        start=(c == 0), stop=(c == DC - 1))
                    deps = [wcast[0][c]]
                    if c == 0:
                        deps.append(tok_xn)
                        if m >= 2:
                            deps.append(q_rd[m - 2])
                    tq = T(em, *deps)
                def s1(e, half=half, m=m):
                    return e.activation(out=sc2bf[:, 0:512],
                                        in_=pW0[:, half * 512:(half + 1) * 512],
                                        func=ACT.Relu, bias=colp[:, CB["bq"] + m:CB["bq"] + m + 1])
                a1 = S(s1, tq, tok_q)
                def s2(e, half=half, m=m):
                    return e.activation(out=sc2[:, 256:768],
                                        in_=pW0[:, half * 512:(half + 1) * 512],
                                        func=ACT.Relu, scale=-1.0,
                                        bias=colp[:, CB["negbq"] + m:CB["negbq"] + m + 1])
                a2 = S(s2, a1)
                a3 = S(lambda e: e.activation(out=sc2bf[:, 1536:2048], in_=sc2[:, 256:768],
                                              func=ACT.Exp, scale=-1.0), a2)
                def v1(e, m=m):
                    return e.tensor_tensor(out=qT[:, m * 512:(m + 1) * 512],
                                           in0=sc2bf[:, 0:512], in1=sc2bf[:, 1536:2048],
                                           op=ALU.add)
                tok_q = V(v1, a3)
                q_rd[m] = a2
            if ti == 0:
                tap("q_t0", qT[:, :], BF16, tok_q)
            den_ev = {}
            zb_tok = {}
            zbd_tok = {}
            o_ev = []
            for c in range(DC):
                half = c % 2
                dhalf = c % 2
                def ed(e, c=c):
                    return e.matmul(out=pDen[:, :], lhsT=ksE[:, 2 * c:2 * c + 2],
                                    rhs=qT[:, c * 512:(c + 1) * 512], start=True, stop=True)
                deps = [tok_q, tok_kv]
                if c >= 1:
                    deps.append(den_ev[c - 1])
                td = T(ed, *deps)
                def ev(e, dhalf=dhalf):
                    return e.tensor_copy(out=sc2[0:2, dhalf * 512:(dhalf + 1) * 512],
                                         in_=pDen[:, :])
                deps2 = [td]
                if c >= 2:
                    deps2.append(zb_tok[c - 2])
                    deps2.append(zbd_tok[c - 2])
                tde = V(ev, *deps2)
                den_ev[c] = tde
                def ez1(e, dhalf=dhalf):
                    return e.activation(out=sc2[0:2, dhalf * 512:(dhalf + 1) * 512],
                                        in_=sc2[0:2, dhalf * 512:(dhalf + 1) * 512],
                                        func=ACT.Ln, bias=colp[0:2, 3:4])
                tz1 = S(ez1, tde)
                def ez2(e, dhalf=dhalf):
                    return e.activation(out=sc2[0:2, dhalf * 512:(dhalf + 1) * 512],
                                        in_=sc2[0:2, dhalf * 512:(dhalf + 1) * 512],
                                        func=ACT.Exp, scale=-1.0)
                tz2 = S(ez2, tz1)
                tmo = None
                for hh in (2 * c, 2 * c + 1):
                    def eo(e, hh=hh, c=c, half=half):
                        return e.matmul(out=pW1[64 * (hh % 2):64 * (hh % 2) + 64,
                                              half * 512:(half + 1) * 512],
                                        lhsT=kvE[64 * (hh % 2):64 * (hh % 2) + 64,
                                                 (hh // 2) * 64:(hh // 2) * 64 + 64],
                                        rhs=qT[64 * (hh % 2):64 * (hh % 2) + 64,
                                               c * 512:(c + 1) * 512],
                                        start=True, stop=True)
                    deps = [tok_kv, tok_q]
                    if len(o_ev) >= 2:
                        deps.append(o_ev[-2])
                    tmo = T(eo, *deps)
                def ezm(e, dhalf=dhalf):
                    return e.matmul(out=pKV[:, 0:512], lhsT=zxsel[0:2, 0:128],
                                    rhs=sc2[0:2, dhalf * 512:(dhalf + 1) * 512],
                                    start=True, stop=True)
                t_zmm = T(ezm, tz2, tok_zxsel, tok_kv, zmm_rd)
                zb_tok[c] = t_zmm
                zbd_tok[c] = t_zmm
                def ezev(e, half=half):
                    return e.activation(out=zb[:, half * 512:(half + 1) * 512],
                                        in_=pKV[:, 0:512], func=ACT.Identity)
                t_zev = S(ezev, t_zmm, o_ev[-2] if len(o_ev) >= 2 else None)
                zmm_rd = t_zev
                def evv(e, c=c, half=half):
                    return e.tensor_tensor(out=qT[:, c * 512:(c + 1) * 512],
                                           in0=pW1[:, half * 512:(half + 1) * 512],
                                           in1=zb[:, half * 512:(half + 1) * 512], op=ALU.mult)
                o_ev.append(V(evv, tmo, t_zev))
            tok_osc = o_ev[-1]
            if ti == 0:
                tap("osc_t0", qT[:, :], BF16, tok_osc)
            wo_rd = {}
            for m in range(DC):
                half = m % 2
                for c in range(DC):
                    def em(e, m=m, c=c, half=half):
                        return e.matmul(out=pW0[:, half * 512:(half + 1) * 512],
                                        lhsT=wchunk(1, c, m * 128, (m + 1) * 128),
                                        rhs=qT[:, c * 512:(c + 1) * 512],
                                        start=(c == 0), stop=(c == DC - 1))
                    deps = [wcast[1][c]]
                    if c == 0:
                        deps.append(tok_osc)
                        deps.append(wo_rd[m - 2] if m >= 2 else q_rd[6 + m])
                    tw = T(em, *deps)
                def s1(e, m=m, half=half):
                    return e.activation(out=sc2[:, 0:512], in_=pW0[:, half * 512:(half + 1) * 512],
                                        func=ACT.Identity,
                                        bias=colp[:, CB["bo"] + m:CB["bo"] + m + 1])
                aw = S(s1, tw, last_wo_ev)
                def v1(e, m=m, o=o):
                    return e.tensor_tensor(out=xbap(m)[:, o:o + TILE], in0=sc2[:, 0:512],
                                           in1=xn_bf[:, m * 512:(m + 1) * 512], op=ALU.add)
                last_wo_ev = V(v1, aw)
                wo_rd[m] = aw
        tap("xa_c0", xbap(0), F32, last_wo_ev)
        prog.barrier()

        rt1 = ln_stats(xbap)
        bc3 = BC(rt1, None)
        tok_w1 = load_weight_1024(0, w1_h, rows0=0, cols0=0, dep0=rt1[1])
        tok_w2 = load_weight_1024(1, w2_h, rows0=0, cols0=0, dep0=rt1[1])
        tok_xa2bf = None
        for ti in range(NT):
            o = ti * TILE
            bc3.load(ti)
            tok_bc = bc3.tok[ti]
            tlast = None
            for c in range(DC):
                d1 = V(lambda e, c=c, o=o, ti=ti: e.tensor_tensor(
                    out=xbap(c)[:, o:o + TILE], in0=xbap(c)[:, o:o + TILE],
                    in1=bc3.nmu(ti), op=ALU.add), tok_bc)
                d2 = V(lambda e, c=c, o=o, ti=ti: e.tensor_tensor(
                    out=xbap(c)[:, o:o + TILE], in0=xbap(c)[:, o:o + TILE],
                    in1=bc3.rsd(ti), op=ALU.mult), d1)
                d3 = V(lambda e, c=c, o=o: e.tensor_scalar(
                    out=xbap(c)[:, o:o + TILE], in0=xbap(c)[:, o:o + TILE],
                    scalar1=colp[:, CB["g1"] + c:CB["g1"] + c + 1],
                    scalar2=None, op0=ALU.mult), d2)
                d4 = V(lambda e, c=c, o=o: e.tensor_scalar(
                    out=xbap(c)[:, o:o + TILE], in0=xbap(c)[:, o:o + TILE],
                    scalar1=colp[:, CB["be1"] + c:CB["be1"] + c + 1],
                    scalar2=None, op0=ALU.add), d3)
                tlast = V(lambda e, c=c, o=o: e.tensor_copy(
                    out=xa2bf(c, o, o + TILE), in_=xbap(c)[:, o:o + TILE]), d4)
            bc3.done(ti, tlast)
            tok_xa2bf = tlast
        tap("xa2_c0", xbap(0), F32, tok_xa2bf)
        tok_xa2_done = tok_xa2bf
        tap("xa2bf0", qT[:, 0:2048], BF16, tok_xa2bf)
        tap("xa2bf4", zbbf[:, 0:2048], BF16, tok_xa2bf)
        tap("xa2bf5", lnD[:, 2048:4096], BF16, tok_xa2bf)
        tap("xa2bf7", sc2bf[:, 0:2048], BF16, tok_xa2bf)

        last_w2ev = None
        eT_war = None
        for jg in range(JG):
            if jg > 0:
                tok_w2 = load_weight_1024(1, w2_h, rows0=jg * 1024, cols0=0)
            for ti in range(NFT):
                o = ti * FT
                h_toks = {}
                tok_h1 = None
                for j in range(8):
                    half = j % 2
                    for c in range(DC):
                        def em(e, j=j, c=c, half=half, o=o):
                            return e.matmul(out=pW0[:, half * 512:half * 512 + FT],
                                            lhsT=wchunk(0, c, j * 128, (j + 1) * 128),
                                            rhs=xa2bf(c, o, o + FT),
                                            start=(c == 0), stop=(c == DC - 1))
                        deps = [wcast[0][c]]
                        if c == 0:
                            deps.append(tok_xa2bf)
                            if j >= 2:
                                deps.append(h_toks[j - 2])
                            elif last_w2ev is not None:
                                deps.append(last_w2ev)
                        T(em, *deps)
                    jglob = jg * 8 + j
                    def s1(e, j=j, half=half, jglob=jglob):
                        return e.activation(out=lnD[:, j * FT:(j + 1) * FT],
                                            in_=pW0[:, half * 512:half * 512 + FT],
                                            func=ACT.Relu,
                                            bias=colp[:, 96 + jglob:96 + jglob + 1])
                    tok_h1 = S(s1, prog.snap("t"))
                    h_toks[j] = tok_h1
                if ti == NFT - 1 and jg + 1 < JG:
                    load_weight_1024(0, w1_h, rows0=0, cols0=(jg + 1) * 1024)
                if jg == 0 and ti == 0:
                    tap("h1_00", lnD[:, 0:2048], BF16, tok_h1)
                for m in range(DC):
                    half = m % 2
                    for j in range(8):
                        def em(e, m=m, j=j, half=half):
                            return e.matmul(out=pW1[:, half * 512:half * 512 + FT],
                                            lhsT=wchunk(1, j, m * 128, (m + 1) * 128),
                                            rhs=lnD[:, j * FT:(j + 1) * FT],
                                            start=(j == 0), stop=(j == 7))
                        deps = [wcast[1][j]]
                        if j == 0:
                            deps.append(tok_h1)
                            if last_w2ev is not None and m >= 2:
                                deps.append(last_w2ev)
                        tw = T(em, *deps)
                    if jg == 0:
                        def sb2(e, m=m, half=half):
                            return e.activation(out=nmu_bc[:, 0:FT],
                                                in_=pW1[:, half * 512:half * 512 + FT],
                                                func=ACT.Identity,
                                                bias=colp[:, CB["b2"] + m:CB["b2"] + m + 1])
                        tsb = S(sb2, tw, last_w2ev)
                        def ev(e, m=m, o=o):
                            return e.tensor_tensor(out=xbap(m)[:, o:o + FT],
                                                   in0=nmu_bc[:, 0:FT],
                                                   in1=xbap(m)[:, o:o + FT], op=ALU.add)
                        last_w2ev = V(ev, tsb)
                    else:
                        def ev(e, m=m, o=o, half=half):
                            return e.tensor_tensor(out=xbap(m)[:, o:o + FT],
                                                   in0=pW1[:, half * 512:half * 512 + FT],
                                                   in1=xbap(m)[:, o:o + FT], op=ALU.add)
                        last_w2ev = V(ev, tw)
            tap("ffn_jg%d" % jg, xbap(0), F32, last_w2ev)
        tap("x2py_c0", xbap(0), F32, last_w2ev)
        prog.barrier()

        rt2 = ln_stats(xbap)
        bc4 = BC(rt2, None)
        for ti in range(NT):
            o = ti * TILE
            bc4.load(ti)
            tok_bc = bc4.tok[ti]
            tlast = None
            for c in range(DC):
                d1 = V(lambda e, c=c, o=o, ti=ti: e.tensor_tensor(
                    out=sc1[:, 0:512], in0=xbap(c)[:, o:o + TILE], in1=bc4.nmu(ti), op=ALU.add),
                    tok_bc)
                d2 = V(lambda e, ti=ti: e.tensor_tensor(
                    out=sc1[:, 0:512], in0=sc1[:, 0:512], in1=bc4.rsd(ti), op=ALU.mult), d1)
                d3 = V(lambda e, c=c: e.tensor_scalar(
                    out=sc1[:, 0:512], in0=sc1[:, 0:512],
                    scalar1=colp[:, CB["g2"] + c:CB["g2"] + c + 1],
                    scalar2=None, op0=ALU.mult), d2)
                d4 = V(lambda e, c=c: e.tensor_scalar(
                    out=sc1[:, 0:512], in0=sc1[:, 0:512],
                    scalar1=colp[:, CB["be2"] + c:CB["be2"] + c + 1],
                    scalar2=None, op0=ALU.add), d3)
                tlast = V(lambda e, c=c, o=o: e.tensor_tensor(
                    out=xap(c)[:, o:o + TILE], in0=xap(c)[:, o:o + TILE], in1=sc1[:, 0:512],
                    op=ALU.add), d4)
            bc4.done(ti, tlast)
        prog.barrier()

        def ternary_prep(rt):
            rtoks = []
            for c in range(DC):
                sl = c % 2
                deps = [rtoks[c - 2]] if c >= 2 else [rt[1], sc1_war[sl]]
                def em(e, c=c, sl=sl):
                    return e.dma_start(out=sc1[:, sl * 1024:(sl + 1) * 1024],
                                       in_=wt_h[c * 128:(c + 1) * 128, :]).then_inc(LD, 16)
                DMA(em, *deps)
                td = prog.snap("ld")
                def er(e, c=c, sl=sl):
                    return e.tensor_reduce(out=colp[:, 136 + c:137 + c],
                                           in_=sc1[:, sl * 1024:(sl + 1) * 1024],
                                           axis=AX.X, op=ALU.add, apply_absolute_value=True)
                rtoks.append(V(er, td))
            tsum = V(lambda e: e.tensor_reduce(out=colp[:, 144:145], in_=colp[:, 136:144],
                                               axis=AX.X, op=ALU.add), rtoks[-1])
            tap("tern_redc", colp[:, 130:146], F32, tsum)
            tmm = T(lambda e: e.matmul(out=pDen[0:1, 0:1], lhsT=colp[:, 131:132],
                                       rhs=colp[:, 144:145],
                                       start=True, stop=True), tsum)
            tev = V(lambda e: e.tensor_scalar(out=colp[0:1, 135:136], in0=pDen[0:1, 0:1],
                                              scalar1=0.7 / (1024.0 * 1024.0), scalar2=None,
                                              op0=ALU.mult), tmm)
            DMA(lambda e: e.dma_start(out=colp[1:2, 135:136],
                                      in_=colp[0:1, 135:136]).then_inc(LD, 16), tev)
            tok_dd = prog.snap("ld")
            tdm = T(lambda e: e.matmul(out=pSt[:, 0:1], lhsT=zxsel[0:2, 0:128],
                                       rhs=colp[0:2, 135:136], start=True, stop=True),
                    tok_dd, pst_war[0], tok_zxsel)
            tok_d = S(lambda e: e.activation(out=colp[:, 130:131], in_=pSt[:, 0:1],
                                             func=ACT.Identity), tdm)
            pst_war[0] = tok_d
            last = None
            qtoks = []
            pq_war = {0: None, 1: None}
            for c in range(DC):
                sl = c % 2
                deps = [rtoks[6 + sl] if c < 2 else None, pq_war[sl]]
                def em(e, c=c, sl=sl):
                    return e.dma_start(out=sc1[:, sl * 1024:(sl + 1) * 1024],
                                       in_=wt_h[c * 128:(c + 1) * 128, :]).then_inc(LD, 16)
                DMA(em, *deps)
                td2 = prog.snap("ld")
                t1 = S(lambda e, sl=sl: e.activation(out=eT[:, 0:1024],
                                                     in_=sc1[:, sl * 1024:(sl + 1) * 1024],
                                                     func=ACT.Sign), td2, last, rt[1])
                t2 = S(lambda e, sl=sl: e.activation(out=qTf32[:, 1024:2048],
                                                     in_=sc1[:, sl * 1024:(sl + 1) * 1024],
                                                     func=ACT.Abs), td2, rt[1])
                pq_war[sl] = [t1, t2]
                t3 = V(lambda e: e.tensor_scalar(out=eT[:, 1024:2048], in0=qTf32[:, 1024:2048],
                                                 scalar1=colp[:, 130:131], scalar2=None,
                                                 op0=ALU.is_gt), t2, tok_d)
                last = V(lambda e, c=c: e.tensor_tensor(out=wchunk(1, c, 0, 1024),
                                                        in0=eT[:, 0:1024],
                                                        in1=eT[:, 1024:2048],
                                                        op=ALU.mult), t1, t3)
                qtoks.append(last)
            sc1_war[0] = last
            sc1_war[1] = last
            return last, qtoks

        def ternary_sublayer(first):
            rt = ln_stats(xap, alpha_scale=True)
            tok_tw, qtoks = ternary_prep(rt) if first else (None, None)
            tap("tern_w", wslot(1)[:, 0:8192], BF16, tok_tw)
            tap("tern_rows", rows2[:, :], F32, rt[0], rt[1])
            tap("tern_delta", colp[:, 130:131], F32, prog.snap("v"))
            bct = BC(rt, None)
            last_ev = None
            for ti in range(NT):
                o = ti * TILE
                bct.load(ti)
                tok_bc = bct.tok[ti]
                tok_xn1 = None
                for c in range(DC):
                    def e1(e, c=c, o=o, ti=ti):
                        return e.tensor_tensor(out=sc2[:, 0:512], in0=xap(c)[:, o:o + TILE],
                                               in1=bct.nmu(ti), op=ALU.add)
                    deps = [tok_bc]
                    if ti > 0 and c == 0:
                        deps.append(last_ev)
                    d1 = V(e1, *deps)
                    def e2(e, c=c, ti=ti):
                        return e.tensor_tensor(out=eT[:, c * 512:(c + 1) * 512],
                                               in0=sc2[:, 0:512],
                                               in1=bct.rsd(ti), op=ALU.mult)
                    tok_xn1 = V(e2, d1)
                bct.done(ti, tok_xn1)
                if ti == 0:
                    tap("tern_xn1", eT[:, :], BF16, tok_xn1)
                ev_rd = {}
                for m in range(DC):
                    half = m % 2
                    for c in range(DC):
                        def em(e, m=m, c=c, half=half):
                            return e.matmul(out=pW0[:, half * 512:(half + 1) * 512],
                                            lhsT=wchunk(1, c, m * 128, (m + 1) * 128),
                                            rhs=eT[:, c * 512:(c + 1) * 512],
                                            start=(c == 0), stop=(c == DC - 1))
                        deps = [qtoks[c] if qtoks is not None else None]
                        if c == 0:
                            deps.append(tok_xn1)
                            if m >= 2:
                                deps.append(ev_rd[m - 2])
                        tb = T(em, *deps)
                    def sbias(e, half=half, m=m):
                        return e.activation(out=pW1[:, half * 512:(half + 1) * 512],
                                            in_=pW0[:, half * 512:(half + 1) * 512],
                                            func=ACT.Identity,
                                            bias=colp[:, CB["biast"] + m:CB["biast"] + m + 1])
                    ts_ = S(sbias, tb, ev_rd.get(m - 2))
                    def ev(e, m=m, o=o, half=half):
                        return e.tensor_tensor(out=xap(m)[:, o:o + TILE],
                                               in0=pW1[:, half * 512:(half + 1) * 512],
                                               in1=xap(m)[:, o:o + TILE], op=ALU.add)
                    last_ev = V(ev, ts_)
                    ev_rd[m] = last_ev
            prog.barrier()

        if N_SUBLAYERS >= 2:
            ternary_sublayer(first=True)

        if N_SUBLAYERS >= 3:
            rt3 = ln_stats(xap)
            war_t = prog.snap("t")
            DMA(lambda e: e.dma_start(out=sc1[:, 0:768], in_=wxp_h).then_inc(LD, 16), rt3[1],
                sc1_war[0])
            tok_wl = prog.snap("ld")
            DMA(lambda e: e.dma_start(out=sc1[0:64, 1024:2048], in_=wdt_h).then_inc(LD, 16),
                sc1_war[1])
            tok_wl2 = prog.snap("ld")
            tok_wxp = V(lambda e: e.tensor_copy(out=wxpv, in_=sc1[:, 0:768]), tok_wl, war_t)
            tok_wdt = V(lambda e: e.tensor_copy(out=wdtv, in_=sc1[0:64, 1024:2048]), tok_wl2)
            sc1_war[0] = tok_wxp
            sc1_war[1] = tok_wdt
            tsel0 = V(lambda e: e.memset(wW[64:96, 2048:6144], 0.0))
            DMA(lambda e: e.dma_start(out=wW[64:65, 2048:2176],
                                      in_=onesr[0:1, 0:128]).then_inc(LD, 16), tsel0)
            for k in (1, 2, 4, 8, 16):
                DMA(lambda e, k=k: e.dma_start(
                    out=wW[64 + k:64 + 2 * k, 2048 + 128 * k:2048 + 256 * k],
                    in_=wW[64:64 + k, 2048:2048 + 128 * k]).then_inc(LD, 16))
            tok_sel = prog.snap("ld")

            def selw(w):
                return wW[64:96, 2048 + 128 * w:2176 + 128 * w]

            bc5 = BC(rt3, None)
            xp_last = None
            for ti in range(NT):
                o = ti * TILE
                bc5.load(ti)
                tok_bc = bc5.tok[ti]
                half = ti % 2
                ln_toks = {}
                for c in range(DC):
                    def e1(e, c=c, o=o, ti=ti):
                        return e.tensor_tensor(out=sc2[:, 0:512], in0=xap(c)[:, o:o + TILE],
                                               in1=bc5.nmu(ti), op=ALU.add)
                    deps = [tok_bc]
                    if ti >= 2 and c == 0:
                        deps.append(xp_last)
                    d1 = V(e1, *deps)
                    def e2(e, c=c, ti=ti):
                        return e.tensor_tensor(out=lnD[:, c * 512:(c + 1) * 512],
                                               in0=sc2[:, 0:512], in1=bc5.rsd(ti), op=ALU.mult)
                    ln_toks[c] = V(e2, d1)
                bc5.done(ti, ln_toks[DC - 1])
                txp = None
                for c in range(DC):
                    def em(e, c=c, half=half):
                        return e.matmul(out=pW0[0:96, half * 512:(half + 1) * 512],
                                        lhsT=wxpv[:, c * 96:(c + 1) * 96],
                                        rhs=lnD[:, c * 512:(c + 1) * 512],
                                        start=(c == 0), stop=(c == DC - 1))
                    deps = [ln_toks[c], tok_wxp]
                    if ti >= 2 and c == 0:
                        deps.append(xp_last)
                    txp = T(em, *deps)
                tv1 = V(lambda e, o=o, half=half: e.tensor_copy(
                    out=dtlow_bf[:, o:o + TILE],
                    in_=pW0[0:64, half * 512:(half + 1) * 512]), txp, rt3[0], rt3[1])
                xp_last = V(lambda e, o=o, half=half: e.tensor_copy(
                    out=zbbf[64:96, o:o + TILE],
                    in_=pW0[64:96, half * 512:(half + 1) * 512]), tv1)
            tok_xp = xp_last
            prog.barrier()

            dtB_ = [xB[:, 0:L], xB[:, 5 * L:6 * L]]
            yB_ = [xB[:, L:L + 1024].bitcast(BF16),
                   xB[:, L + 1024:2 * L].bitcast(BF16)]
            AbB_ = [xB[:, 2 * L:3 * L], xB[:, 3 * L:4 * L]]
            hB_ = [xB[:, 4 * L:4 * L + 1024].bitcast(BF16),
                   xB[:, 4 * L + 1024:5 * L].bitcast(BF16)]
            tokB_ = [xB[:, 6 * L:6 * L + 1024].bitcast(BF16),
                     xB[:, 6 * L + 1024:7 * L].bitcast(BF16)]
            prB_ = [xB[:, 7 * L:7 * L + 1024].bitcast(BF16),
                    xB[:, 7 * L + 1024:8 * L].bitcast(BF16)]
            ln2c = lnD[:, 0:4096].bitcast(F32)
            uB = lnD[:, 4096:6144]
            zcol = sc2[:, 1023:1024]
            tz = V(lambda e: e.memset(zcol, 0.0), rt3[0])
            tap("Acol_t", Acol[:, :], F32, tz)
            bc6 = BC(rt3, None)
            slab_last = [None, None]
            pw0_rd = {0: None, 1: None}
            pw1_rd = {0: None, 1: None}
            scan_done = [None, None]
            prod_done = [None, None]
            yadd_done = [None, None]
            dt_rd = {}
            bc6_cnt = [0]
            dt_war = [None, None]
            y_war = [None, None]
            sp_war = [None, None]
            st = {}

            def prep_chunk(c):
                buf = c % 2
                dtB = dtB_[buf]
                yB = yB_[buf]
                ln_tok = None
                tok_dt = None
                for ti in range(NT):
                    o = ti * TILE
                    bidx = bc6_cnt[0]; bc6_cnt[0] += 1
                    bc6.load(bidx)
                    tok_bc = bc6.tok[bidx]
                    def e1(e, c=c, o=o, bidx=bidx):
                        return e.tensor_tensor(out=sc2[:, 0:512], in0=xap(c)[:, o:o + TILE],
                                               in1=bc6.nmu(bidx), op=ALU.add)
                    d1 = V(e1, tok_bc)
                    yield
                    def e2(e, o=o, bidx=bidx):
                        return e.tensor_tensor(out=ln2c[:, o:o + TILE],
                                               in0=sc2[:, 0:512], in1=bc6.rsd(bidx), op=ALU.mult)
                    ln_tok = V(e2, d1)
                    bc6.done(bidx, ln_tok)
                    idx = c * NT + ti
                    b0 = (ti % 2) * 1024
                    def em(e, c=c, o=o):
                        return e.matmul(out=pKV[:, 0:512],
                                        lhsT=wdtv[:, c * 128:(c + 1) * 128],
                                        rhs=dtlow_bf[:, o:o + TILE],
                                        start=True, stop=True)
                    deps = [tok_xp, tok_wdt]
                    if idx >= 1:
                        deps.append(dt_rd[idx - 1])
                    tm = T(em, *deps)
                    def s1(e, c=c, b0=b0):
                        return e.activation(out=sc1[:, b0:b0 + 512],
                                            in_=pKV[:, 0:512],
                                            func=ACT.Identity,
                                            bias=colp[:, CB["bdt"] + c:CB["bdt"] + c + 1])
                    a1 = S(s1, tm, sp_war[ti % 2])
                    dt_rd[idx] = a1
                    a2 = S(lambda e, b0=b0: e.activation(out=sc1[:, b0 + 512:b0 + 1024],
                                                         in_=sc1[:, b0:b0 + 512],
                                                         func=ACT.Abs), a1)
                    a3 = S(lambda e, b0=b0: e.activation(out=sc1[:, b0 + 512:b0 + 1024],
                                                         in_=sc1[:, b0 + 512:b0 + 1024],
                                                         func=ACT.Exp, scale=-1.0), a2)
                    a4 = S(lambda e, b0=b0: e.activation(out=sc1[:, b0 + 512:b0 + 1024],
                                                         in_=sc1[:, b0 + 512:b0 + 1024],
                                                         func=ACT.Ln, bias=colp[:, 129:130]), a3)
                    a5 = S(lambda e, b0=b0: e.activation(out=sc1[:, b0:b0 + 512],
                                                         in_=sc1[:, b0:b0 + 512],
                                                         func=ACT.Relu), a4)
                    def v1(e, o=o, b0=b0, dtB=dtB):
                        return e.tensor_tensor(out=dtB[:, o:o + TILE], in0=sc1[:, b0:b0 + 512],
                                               in1=sc1[:, b0 + 512:b0 + 1024], op=ALU.add)
                    tok_dt = V(v1, a5, dt_war[buf])
                    sp_war[ti % 2] = tok_dt
                    yield
                def ey(e, c=c, yB=yB):
                    return e.tensor_scalar(out=yB, in0=ln2c,
                                           scalar1=colp[:, CB["dvec"] + c:CB["dvec"] + c + 1],
                                           scalar2=None, op0=ALU.mult)
                tok_y = V(ey, ln_tok, tok_dt, y_war[buf])
                st[c] = (tok_dt, ln_tok, tok_y)
                yield

            g = prep_chunk(0)
            for _ in g:
                pass
            for c in range(DC):
                cbuf = c % 2
                dtB = dtB_[cbuf]
                yB = yB_[cbuf]
                tok_dt, ln_tok, tok_y = st[c]
                g = prep_chunk(c + 1) if c + 1 < DC else None
                def eu(e, dtB=dtB):
                    return e.tensor_tensor(out=uB, in0=dtB, in1=ln2c, op=ALU.mult)
                tok_u = V(eu, tok_dt, ln_tok)
                prev = None
                ta = None
                tt_ = None
                for si in range(DS):
                    buf = si % 2
                    AbB = AbB_[buf]; tokB = tokB_[buf]; hB = hB_[buf]; prB = prB_[buf]
                    bmslab = qT[:, buf * 2048:(buf + 1) * 2048]
                    cmslab = eT[:, buf * 2048:(buf + 1) * 2048]
                    tok_slab = None
                    tok_cslab = None
                    for q in range(4):
                        h2 = q % 2
                        def emb(e, si=si, q=q, h2=h2):
                            return e.matmul(out=pW0[:, h2 * 512:(h2 + 1) * 512],
                                            lhsT=selw(si),
                                            rhs=zbbf[64:96, q * 512:(q + 1) * 512],
                                            start=True, stop=True)
                        tb = T(emb, tok_xp, tok_sel, pw0_rd[h2])
                        def esb(e, q=q, h2=h2, bmslab=bmslab):
                            return e.activation(out=bmslab[:, q * 512:(q + 1) * 512],
                                                in_=pW0[:, h2 * 512:(h2 + 1) * 512],
                                                func=ACT.Identity)
                        tok_slab = S(esb, tb, slab_last[buf])
                        pw0_rd[h2] = tok_slab
                    for q in range(4):
                        h2 = q % 2
                        def emc(e, si=si, q=q, h2=h2):
                            return e.matmul(out=pW1[:, h2 * 512:(h2 + 1) * 512],
                                            lhsT=selw(16 + si),
                                            rhs=zbbf[64:96, q * 512:(q + 1) * 512],
                                            start=True, stop=True)
                        tc = T(emc, tok_xp, tok_sel, pw1_rd[h2])
                        def esc(e, q=q, h2=h2, cmslab=cmslab):
                            return e.activation(out=cmslab[:, q * 512:(q + 1) * 512],
                                                in_=pW1[:, h2 * 512:(h2 + 1) * 512],
                                                func=ACT.Identity)
                        tok_cslab = S(esc, tc, slab_last[buf])
                        pw1_rd[h2] = tok_cslab
                    def ea(e, c=c, si=si, AbB=AbB, dtB=dtB):
                        return e.activation(out=AbB, in_=dtB, func=ACT.Exp,
                                            scale=Acol[:, c * 16 + si:c * 16 + si + 1])
                    ta = S(ea, tok_dt, scan_done[buf])
                    def et(e, bmslab=bmslab, tokB=tokB):
                        return e.tensor_tensor(out=tokB, in0=uB, in1=bmslab, op=ALU.mult)
                    tt_ = V(et, tok_u, tok_slab, scan_done[buf])
                    def eh(e, AbB=AbB, tokB=tokB, hB=hB):
                        return e.tensor_tensor_scan(out=hB, data0=AbB, data1=tokB,
                                                    initial=zcol, op0=ALU.mult, op1=ALU.add)
                    th = V(eh, ta, tt_, tz, prod_done[buf])
                    scan_done[buf] = th
                    def ep(e, cmslab=cmslab, hB=hB, prB=prB):
                        return e.tensor_tensor(out=prB, in0=hB, in1=cmslab, op=ALU.mult)
                    tp = V(ep, th, yadd_done[buf], tok_cslab)
                    prod_done[buf] = tp
                    slab_last[buf] = tp
                    def ey2(e, prB=prB, yB=yB):
                        return e.tensor_tensor(out=yB, in0=yB, in1=prB, op=ALU.add)
                    prev = V(ey2, tp, tok_y, prev)
                    yadd_done[buf] = prev
                    if g is not None:
                        next(g, None)
                if g is not None:
                    for _ in g:
                        pass
                dt_war[cbuf] = ta
                txa = V(lambda e, c=c, yB=yB: e.tensor_tensor(out=xap(c), in0=xap(c), in1=yB,
                                                              op=ALU.add), prev)
                y_war[cbuf] = txa
            prog.barrier()

        if N_SUBLAYERS >= 4:
            ternary_sublayer(first=False)

        for c in range(DC):
            DMAO(lambda e, c=c, seq=seq: e.dma_start(
                out=out_h[seq, c], in_=xA[:, c * L:(c + 1) * L]).then_inc(SO, 16))
        prog.barrier()

    n_so = prog.cnt["so"]
    prog.plan["y"].append(([("so", n_so)], lambda e: e.nop(), None))

    with nc.Block() as block:
        def make(engname):
            def body(eng):
                wm = {}
                for deps, emit, inctok in prog.plan[engname]:
                    for (sn, val) in deps:
                        if val > wm.get(sn, 0):
                            eng.wait_ge(sems[sn], val)
                            wm[sn] = val
                    inst = emit(eng)
                    if inctok is not None:
                        inst.then_inc(sems[inctok[0]], 1)
            return body
        block.vector(make("v"))
        block.scalar(make("s"))
        block.tensor(make("t"))
        if prog.plan["g"]:
            block.gpsimd(make("g"))
        block.sync(make("y"))

    for h in reversed(ctx):
        h.__exit__(None, None, None)
    return nc


_NC_CACHE = {}


def kernel(**inputs):
    if "nc" not in _NC_CACHE:
        _NC_CACHE["nc"] = build_nc()
    nc = _NC_CACHE["nc"]

    x = np.asarray(inputs["L0_attn_a_x"])
    asc = np.ascontiguousarray
    f32 = np.float32
    shared = {
        "wq": asc(np.asarray(inputs["L0_attn_a_Wq"], dtype=f32)),
        "wk": asc(np.asarray(inputs["L0_attn_a_Wk"], dtype=f32)),
        "wv": asc(np.asarray(inputs["L0_attn_a_Wv"], dtype=f32)),
        "wo": asc(np.asarray(inputs["L0_attn_a_Wo"], dtype=f32)),
        "w1": asc(np.asarray(inputs["L0_attn_a_W1"], dtype=f32)),
        "w2": asc(np.asarray(inputs["L0_attn_a_W2"], dtype=f32)),
        "wt": asc(np.asarray(inputs["L0_ffn_weight"], dtype=f32).T),
        "wxp": asc(np.asarray(inputs["L0_attn_b_W_xproj"], dtype=f32).T
                   .reshape(8, 128, 96).transpose(1, 0, 2).reshape(128, 768)),
        "wdt": asc(np.asarray(inputs["L0_attn_b_W_dt"], dtype=f32).T),
        "alog": asc(np.asarray(inputs["L0_attn_b_A_log"], dtype=f32)
                    .reshape(8, 128, 16).transpose(1, 0, 2).reshape(128, 128)),
        "bq": asc(np.asarray(inputs["L0_attn_a_bq"], dtype=f32).reshape(8, 128).T),
        "bk": asc(np.asarray(inputs["L0_attn_a_bk"], dtype=f32)),
        "bv": asc(np.asarray(inputs["L0_attn_a_bv"], dtype=f32)),
        "bo": asc(np.asarray(inputs["L0_attn_a_bo"], dtype=f32).reshape(8, 128).T),
        "b1": asc(np.asarray(inputs["L0_attn_a_b1"], dtype=f32).reshape(32, 128).T),
        "b2": asc(np.asarray(inputs["L0_attn_a_b2"], dtype=f32).reshape(8, 128).T),
        "g1": asc(np.asarray(inputs["L0_attn_a_g1"], dtype=f32).reshape(8, 128).T),
        "be1": asc(np.asarray(inputs["L0_attn_a_beta1"], dtype=f32).reshape(8, 128).T),
        "g2": asc(np.asarray(inputs["L0_attn_a_g2"], dtype=f32).reshape(8, 128).T),
        "be2": asc(np.asarray(inputs["L0_attn_a_beta2"], dtype=f32).reshape(8, 128).T),
        "bdt": asc(np.asarray(inputs["L0_attn_b_b_dt"], dtype=f32).reshape(8, 128).T),
        "dvec": asc(np.asarray(inputs["L0_attn_b_D"], dtype=f32).reshape(8, 128).T),
        "biast": asc(np.asarray(inputs["L0_ffn_bias"], dtype=f32).reshape(8, 128).T),
        "alpha": asc(np.asarray(inputs["L0_ffn_alpha"], dtype=f32)),
    }
    in_maps = []
    for k in range(NCORES):
        xs = asc(x[2 * k:2 * k + 2].transpose(0, 2, 1)).reshape(NSEQ, DC, 128, L)
        m = dict(shared)
        m["x"] = xs
        in_maps.append(m)
    res = run_bass_kernel_spmd(nc, in_maps, core_ids=list(range(NCORES)))
    if DEBUG:
        global DBG_OUT
        DBG_OUT = {k: [np.asarray(res.results[i][k]) for i in range(NCORES)]
                   for k in res.results[0] if k.startswith("dbg_")}
    outs = []
    for k in range(NCORES):
        o = np.asarray(res.results[k]["o"]).reshape(NSEQ, D, L).transpose(0, 2, 1)
        outs.append(o)
    return np.concatenate(outs, axis=0).astype(np.float32)
